# version 12
# baseline (speedup 1.0000x reference)
import math
import numpy as np
import concourse.bass as bass
import concourse.mybir as mybir
from concourse.bass_utils import run_bass_kernel_spmd

F32 = mybir.dt.float32
BF16 = mybir.dt.bfloat16
AF = mybir.ActivationFunctionType
ALU = mybir.AluOpType
AXX = mybir.AxisListType.X

S = 2048
D = 1024
NH = 16
E = 8
HH = 4096
DC = 8
NG = 4
NT = 16
PI = math.pi


class Sem:
    def __init__(self, nc, name):
        self.h = nc.alloc_semaphore(name)
        self.n = 0


class Prog:
    ENGINES = ("sync", "gpsimd", "tensor", "scalar", "vector")

    def __init__(self, chain_sems=None):
        self.steps = {e: [] for e in self.ENGINES}
        self.chain_sems = chain_sems or {}
        self.last_tok = {}

    def op(self, eng, fn, waits=(), inc=None, amt=1):
        waits = [t for t in waits if t is not None]
        cs = self.chain_sems.get(eng)
        if cs is not None:
            if eng in self.last_tok:
                waits = waits + [self.last_tok[eng]]
            if inc is None:
                inc = cs
                amt = 1
        w = [(s.h, v) for (s, v) in waits if s is not None]
        incs = []
        if inc is not None:
            inc.n += amt
            incs.append((inc.h, amt))
        self.steps[eng].append((fn, w, incs))
        tok = (inc, inc.n) if inc is not None else None
        if cs is not None and tok is not None and tok[0] is cs:
            self.last_tok[eng] = tok
        return tok

    def dma(self, eng, fn, waits=(), inc=None):
        return self.op(eng, fn, waits, inc, amt=16)

    def emit(self, nc):
        with nc.Block() as blk:
            for ename in self.ENGINES:
                steps = self.steps[ename]
                if not steps:
                    continue

                def mk(steps):
                    def f(eng):
                        for fn, waits, incs in steps:
                            for sem, val in waits:
                                eng.wait_ge(sem, val)
                            inst = fn(eng)
                            for sem, amt in incs:
                                inst.then_inc(sem, amt)
                    return f
                getattr(blk, ename)(mk(steps))


def build_program(stop_after=None, dump=None):
    nc = bass.Bass("TRN2", target_bir_lowering=False,
                   detect_race_conditions=False)

    xT = nc.dram_tensor("xT", [D, S], F32, kind="ExternalInput")
    aqT = nc.dram_tensor("aqT", [D, D], F32, kind="ExternalInput")
    akT = nc.dram_tensor("akT", [D, D], F32, kind="ExternalInput")
    avT = nc.dram_tensor("avT", [D, D], F32, kind="ExternalInput")
    aoT = nc.dram_tensor("aoT", [D, D], F32, kind="ExternalInput")
    bqc_d = nc.dram_tensor("bqc", [128, 8], F32, kind="ExternalInput")
    bkc_d = nc.dram_tensor("bkc", [128, 8], F32, kind="ExternalInput")
    bvc_d = nc.dram_tensor("bvc", [128, 8], F32, kind="ExternalInput")
    boc_d = nc.dram_tensor("boc", [128, 8], F32, kind="ExternalInput")
    anw_d = nc.dram_tensor("anwc", [128, 8], F32, kind="ExternalInput")
    mnw_d = nc.dram_tensor("mnwc", [128, 8], F32, kind="ExternalInput")
    wqT = nc.dram_tensor("wqT", [D, D], F32, kind="ExternalInput")
    wkT = nc.dram_tensor("wkT", [D, D], F32, kind="ExternalInput")
    wvT = nc.dram_tensor("wvT", [D, D], F32, kind="ExternalInput")
    woT = nc.dram_tensor("woT", [D, D], F32, kind="ExternalInput")
    gwT = nc.dram_tensor("gwT", [D, E], F32, kind="ExternalInput")
    w1T = nc.dram_tensor("w1T", [E, D, HH], F32, kind="ExternalInput")
    w2T = nc.dram_tensor("w2T", [E, HH, D], F32, kind="ExternalInput")
    b1c_d = nc.dram_tensor("b1c", [128, E, 32], F32, kind="ExternalInput")
    b2_d = nc.dram_tensor("b2", [E, D], F32, kind="ExternalInput")

    outT = nc.dram_tensor("outT", [D, S], F32, kind="ExternalOutput")
    x1d = nc.dram_tensor("x1d", [DC, 128, S], F32, kind="Internal")
    yd = nc.dram_tensor("yd", [DC, 128, S], F32, kind="Internal")
    wscr = nc.dram_tensor("wscr", [S, E], F32, kind="Internal")
    nscr = nc.dram_tensor("nscr", [2, 16], F32, kind="Internal")

    A = nc.alloc_sbuf_tensor
    ones_f = A("ones_f", [1, 128], F32)
    ones_b = A("ones_b", [1, 128], BF16)
    ones_c = A("ones_c", [128, 1], F32)
    ones_cb = A("ones_cb", [128, 1], BF16)
    bqc = A("bqc_s", [128, 8], F32)
    bkc = A("bkc_s", [128, 8], F32)
    bvc_b = A("bvc_b", [128, 8], BF16)
    boc = A("boc_s", [128, 8], F32)
    anwc = A("anwc_s", [128, 8], F32)
    mnwc = A("mnwc_s", [128, 8], F32)
    bop = A("bop", [128, 8], F32)
    b1c = A("b1c_s", [128, E, 32], F32)
    gw_b = A("gw_b", [128, 8, E], BF16)
    ind = A("ind", [128, DC, 16], BF16)
    negcA = A("negcA", [128, 16], F32)
    negcB = A("negcB", [128, 16], F32)
    negcPA = A("negcPA", [128, 8], F32)
    negcPB = A("negcPB", [128, 8], F32)
    negc16 = A("negc16", [16, 1], F32)
    q16 = A("q16", [16, 1], F32)
    k16 = A("k16", [16, 1], F32)
    t16 = A("t16", [16, 1], F32)
    ncrow = A("ncrow", [1, 16], F32)
    mrow = A("mrow", [1, 512], F32)
    r2row = A("r2row", [1, 512], F32)
    r3row = A("r3row", [1, 512], F32)
    vrow = A("vrow", [1, 512], F32)
    vrowW = A("vrowW", [1, 1024], F32)
    invf = A("invf", [1, 128], F32)
    nsgn = A("nsgn", [128, 1], F32)
    eps5c = A("eps5c", [128, 1], F32)
    eps6c = A("eps6c", [128, 1], F32)
    npic = A("npic", [128, 1], F32)
    max8 = A("max8", [128, 8], F32)
    scr8 = A("scr8", [128, 8], F32)
    L2s = A("L2s", [128, 8], F32)
    L3s = A("L3s", [128, 8], F32)
    n1s = A("n1s", [128, 8], F32)
    n2s = A("n2s", [128, 8], F32)
    w2s = A("w2s", [128, 8], F32)
    ones8f = A("ones8f", [128, 8], F32)
    d12 = A("d12", [128, 1], F32)
    rw1 = A("rw1", [128, 1], F32)
    rw2 = A("rw2", [128, 1], F32)
    logits = A("logits", [128, NT, 8], F32)
    tmpG = A("tmpG", [128, 4, 512], BF16)
    wtm = A("wtm", [128, NT, 8], F32)

    base0 = (nc.sbuf_base + 63) & ~63
    KB = 1024

    def AT(name, shape, dtype, off):
        return nc.alloc_sbuf_tensor_at(name, shape, dtype, offset=base0 + off)

    act_bf = AT("act_bf", [128, DC, S], BF16, 0)
    ctxT = AT("ctxT", [128, DC, S], BF16, 0)
    wA = AT("wA", [128, DC, D], BF16, 32 * KB)
    hid = AT("hid", [128, 2, 4, 512], BF16, 32 * KB)
    wB = AT("wB", [128, DC, D], BF16, 48 * KB)
    cosT = AT("cosT", [128, S], F32, 48 * KB)
    sinT = AT("sinT", [128, S], F32, 56 * KB)
    wem = AT("wem", [E, S], BF16, 48 * KB)
    b2b = AT("b2b", [E, D], BF16, 52 * KB)
    wrow = AT("wrow", [1, S], BF16, 54 * KB)
    wb_sb = AT("wb_sb", [128, 4, 512], BF16, 58 * KB)
    qT = AT("qT", [128, DC, S], BF16, 64 * KB)
    ao_sb = AT("ao_sb", [128, DC, S], BF16, 64 * KB)
    kT = AT("kT", [128, DC, S], BF16, 96 * KB)
    h_sb = AT("h_sb", [128, DC, S], F32, 64 * KB)
    v_sb = AT("v_sb", [128, NT, NH, 65], BF16, 128 * KB)
    w1c = AT("w1c", [128, 2, DC, 512], BF16, 128 * KB)
    w2c = AT("w2c", [128, 2, 4, D], BF16, 144 * KB)
    SC = 166 * KB
    xs = AT("xs", [128, 2, 512], F32, SC)
    sqb = AT("sqb", [128, 2, 512], BF16, SC + 4096)
    tmp1 = AT("tmp1", [128, 512], F32, SC + 6144)
    tmp2 = AT("tmp2", [128, 512], F32, SC + 8192)
    tmpO = AT("tmpO", [128, 2, 512], F32, SC + 10240)
    P_sb = AT("P_sb", [128, 4, 2, 512], BF16, SC)
    masks = AT("masks", [128, 4, 512], BF16, SC + 10240)
    rb_sb = AT("rb_sb", [64, 512], F32, SC + 8192)
    w32a = AT("w32a", [128, 512], F32, SC + 6144)
    w32b = AT("w32b", [128, 512], F32, SC + 8192)
    w32a3 = w32a[:].rearrange("p (t i) -> p t i", i=32)
    w32b3 = w32b[:].rearrange("p (t i) -> p t i", i=32)
    rot = AT("rot", [128, S], BF16, SC)
    rpA = AT("rpA", [128, 512], BF16, SC + 4096)
    rpB = AT("rpB", [128, 512], BF16, SC + 5120)
    assert base0 + SC + 14336 <= nc.sbuf_top, (base0, nc.sbuf_top)

    pbW0 = nc.alloc_psum_tensor("pbW0", [128, 1024], F32)
    pbW1 = nc.alloc_psum_tensor("pbW1", [128, 1024], F32)
    pbW2 = nc.alloc_psum_tensor("pbW2", [128, 1024], F32)
    pb = [pbW0[:, 0:512], pbW0[:, 512:1024],
          pbW1[:, 0:512], pbW1[:, 512:1024],
          pbW2[:, 0:512], pbW2[:, 512:1024]] + [
        nc.alloc_psum_tensor(f"pb{i}", [128, 512], F32) for i in range(6, 8)]

    sA = Sem(nc, "sA")
    sG = Sem(nc, "sG")
    sGc = Sem(nc, "sGc")
    sP = Sem(nc, "sP")
    sS = Sem(nc, "sS")
    sV = Sem(nc, "sV")
    sO = Sem(nc, "sO")

    dump_map = {}

    def mkprog():
        return Prog({"vector": sV, "gpsimd": sGc})

    class St:
        pass

    def ln_stats(p, src_fn, src_bf, mean_bank, ssq_bank, first_waits,
                 sq_guard, dram_fn=None, g=0, xs_state=None):
        mm2_hist = []
        local_toks = {}
        for c in range(DC):
            if dram_fn is not None:
                buf = c % 2
                w = list(xs_state.get(buf, []))
                if c >= 2:
                    w += local_toks[c - 2]
                tok = p.dma("sync", lambda e, c=c, buf=buf: e.dma_start(
                    xs[:, buf, :], dram_fn(c)), waits=w, inc=sA)
                ap, rdy = xs[:, buf, :], tok
            else:
                ap, rdy = src_fn(c)
            w_sq = []
            if rdy:
                w_sq.append(rdy)
            if c == 0:
                w_sq += first_waits
            if c >= 2:
                w_sq.append(mm2_hist[c - 2])
            elif sq_guard[c % 2] is not None:
                w_sq.append(sq_guard[c % 2])
            t_sq = p.op("scalar", lambda e, ap=ap, c=c: e.activation(
                sqb[:, c % 2, :], ap, AF.Square), waits=w_sq, inc=sS)
            w_mm = ([rdy] if rdy else []) + (first_waits if c == 0 else [])
            lhs = ones_cb if src_bf else ones_c
            t1 = p.op("tensor", lambda e, ap=ap, c=c, lhs=lhs: e.matmul(
                mean_bank[0:1, :], lhs[:], ap,
                start=(c == 0), stop=(c == DC - 1)), waits=w_mm, inc=sP)
            t2 = p.op("tensor", lambda e, c=c: e.matmul(
                ssq_bank[0:1, :], ones_cb[:], sqb[:, c % 2, :],
                start=(c == 0), stop=(c == DC - 1)), waits=[t_sq], inc=sP)
            mm2_hist.append(t2)
            local_toks[c] = [t1, t_sq]
            if dram_fn is not None and xs_state is not None:
                xs_state[c % 2] = [t1, t_sq]
            sq_guard[c % 2] = t2
        return mm2_hist[-1], local_toks

    def ln_rows(p, mean_bank, ssq_bank, stats_tok, eps_ap, row_guard):
        w0 = [stats_tok] + [t for t in row_guard if t]
        p.op("vector", lambda e: e.tensor_scalar(
            mrow[:], mean_bank[0:1, :], 1.0 / D, None, ALU.mult),
            waits=w0, inc=sV)
        p.op("vector", lambda e: e.tensor_scalar(
            r2row[:], ssq_bank[0:1, :], 1.0 / D, None, ALU.mult), inc=sV)
        p.op("vector", lambda e: e.tensor_tensor(
            r3row[:], mrow[:], mrow[:], ALU.mult), inc=sV)
        tv = p.op("vector", lambda e: e.tensor_tensor(
            r2row[:], r2row[:], r3row[:], ALU.subtract), inc=sV)
        ts = p.op("scalar", lambda e: e.activation(
            r2row[:], r2row[:], AF.Sqrt, bias=eps_ap[0:1, :]),
            waits=[tv], inc=sS)
        return p.op("vector", lambda e: e.reciprocal(r2row[:], r2row[:]),
                    waits=[ts], inc=sV)

    def bcast(p, bank, row_ap, waits, m=128):
        return p.op("tensor", lambda e, bank=bank, row_ap=row_ap: e.matmul(
            bank[0:m, :], ones_f[0:1, 0:m], row_ap, start=True, stop=True),
            waits=waits, inc=sP)

    def proj_fm(p, w_sb, act, out_cb, w_ready, act_tok, prior, bias=None):
        banks = [pb[0], pb[1], pb[2], pb[3]]
        evac = []
        gi = 0
        for oc in range(DC):
            for g in range(NG):
                bank = banks[gi % 4]
                waits = [w_ready, act_tok]
                waits.append(evac[gi - 4] if gi >= 4 else
                             (prior[gi % 4] if gi < len(prior) else None))
                for ic in range(DC):
                    tok = p.op("tensor", lambda e, ic=ic, oc=oc, g=g,
                               bank=bank: e.matmul(
                        bank[:], w_sb[:, ic, oc * 128:(oc + 1) * 128],
                        act[:, ic, g * 512:(g + 1) * 512],
                        start=(ic == 0), stop=(ic == DC - 1)),
                        waits=(waits if ic == 0 else []), inc=sP)
                dst = out_cb(oc, g)
                if bias is not None:
                    et = p.op("scalar", lambda e, dst=dst, bank=bank, oc=oc:
                              e.activation(dst, bank[:], AF.Identity,
                                           bias=bias[:, oc:oc + 1]),
                              waits=[tok], inc=sS)
                else:
                    et = p.op("scalar", lambda e, dst=dst, bank=bank:
                              e.activation(dst, bank[:], AF.Copy),
                              waits=[tok], inc=sS)
                evac.append(et)
                gi += 1
        return evac

    def proj_tm_v(p, w_sb, act, w_ready, act_tok, prior):
        evac = []
        gi = 0
        banks = [pb[0], pb[1], pb[2], pb[3]]
        for tt in range(NT):
            for dvg in range(2):
                bank = banks[gi % 4]
                waits = [w_ready, act_tok]
                waits.append(evac[gi - 4] if gi >= 4 else
                             (prior[gi % 4] if gi < len(prior) else None))
                for ic in range(DC):
                    tok = p.op("tensor", lambda e, ic=ic, tt=tt, dvg=dvg,
                               bank=bank: e.matmul(
                        bank[:], act[:, ic, tt * 128:(tt + 1) * 128],
                        w_sb[:, ic, dvg * 512:(dvg + 1) * 512],
                        start=(ic == 0), stop=(ic == DC - 1)),
                        waits=(waits if ic == 0 else []), inc=sP)
                et = p.op("vector", lambda e, tt=tt, dvg=dvg, bank=bank:
                          e.tensor_copy(
                    v_sb[:, tt, dvg * 8:(dvg + 1) * 8, 0:64],
                    bank[:].rearrange("p (h d) -> p h d", d=64)),
                    waits=[tok], inc=sV)
                evac.append(et)
                gi += 1
        return evac

    def ph_init():
        p = mkprog()
        for dst, src in ((bqc, bqc_d), (bkc, bkc_d), (boc, boc_d),
                         (anwc, anw_d), (mnwc, mnw_d)):
            p.dma("sync", lambda e, d=dst, s=src: e.dma_start(d[:], s.ap()),
                  inc=sA)
        p.dma("sync", lambda e: e.dma_start(b1c[:], b1c_d.ap()), inc=sA)
        p.dma("gpsimd", lambda e: e.dma_start(
            wA[:], aqT.ap().rearrange("(c p) o -> p c o", p=128)), inc=sG)
        p.dma("gpsimd", lambda e: e.dma_start(
            wB[:], akT.ap().rearrange("(c p) o -> p c o", p=128)), inc=sG)
        p.dma("gpsimd", lambda e: e.dma_start(bvc_b[:], bvc_d.ap()), inc=sG)
        p.dma("gpsimd", lambda e: e.dma_start(
            gw_b[:], gwT.ap().rearrange("(c p) e -> p c e", p=128)), inc=sG)
        p.op("gpsimd", lambda e: e.memset(ones_f[:], 1.0))
        p.op("gpsimd", lambda e: e.memset(ones_b[:], 1.0))
        p.op("gpsimd", lambda e: e.memset(ones_c[:], 1.0))
        p.op("gpsimd", lambda e: e.memset(ones_cb[:], 1.0))
        p.op("gpsimd", lambda e: e.memset(ones8f[:], 1.0))
        p.op("gpsimd", lambda e: e.memset(ind[:], 0.0))
        for c in range(DC):
            p.op("gpsimd", lambda e, c=c: e.memset(
                ind[0:64, c, 2 * c:2 * c + 1], 1.0))
            p.op("gpsimd", lambda e, c=c: e.memset(
                ind[64:128, c, 2 * c + 1:2 * c + 2], 1.0))
        p.op("gpsimd", lambda e: e.memset(v_sb[:, :, :, 64:65], 1.0))
        p.op("gpsimd", lambda e: e.memset(eps5c[:], 1e-5))
        p.op("gpsimd", lambda e: e.memset(eps6c[:], 1e-6))
        p.op("gpsimd", lambda e: e.memset(npic[:], -PI))
        p.op("gpsimd", lambda e: e.memset(nsgn[0:32, :], 1.0))
        p.op("gpsimd", lambda e: e.memset(nsgn[32:64, :], -1.0))
        p.op("gpsimd", lambda e: e.memset(nsgn[64:96, :], 1.0))
        p.op("gpsimd", lambda e: e.memset(nsgn[96:128, :], -1.0), inc=sGc)
        return p

    def ph_ln1():
        p = mkprog()
        sq_guard = [None, None]
        row_guard = [None, None]
        xs_rd = {}
        apply_guard = {0: None, 1: None}
        for g in range(NG):
            st, _ = ln_stats(
                p, None, False, pb[0], pb[1], [], sq_guard,
                dram_fn=lambda c, g=g: xT.ap()[c * 128:(c + 1) * 128,
                                               g * 512:(g + 1) * 512],
                g=g, xs_state=xs_rd)
            rt = ln_rows(p, pb[0], pb[1], st, eps5c, row_guard)
            tb_m = bcast(p, pb[6], mrow[:], [rt, apply_guard[0]])
            tb_r = bcast(p, pb[7], r2row[:], [rt, apply_guard[1]])
            row_guard = [tb_m, tb_r]
            last_t2 = None
            for c in range(DC):
                buf = c % 2
                tok = p.dma("sync", lambda e, c=c, g=g, buf=buf: e.dma_start(
                    xs[:, buf, :], xT.ap()[c * 128:(c + 1) * 128,
                                           g * 512:(g + 1) * 512]),
                    waits=xs_rd.get(buf, []), inc=sA)
                t1 = p.op("vector", lambda e, buf=buf: e.tensor_tensor(
                    tmp1[:], xs[:, buf, :], pb[6][:], ALU.subtract),
                    waits=[tok, tb_m, tb_r], inc=sV)
                xs_rd[buf] = [t1]
                last_t2 = p.op("vector", lambda e, c=c, g=g: e.tensor_tensor(
                    act_bf[:, c, g * 512:(g + 1) * 512], tmp1[:], pb[7][:],
                    ALU.mult), inc=sV)
            apply_guard = {0: last_t2, 1: last_t2}
        dump_map["t_bf"] = (act_bf, [128, DC, S], BF16)
        return p

    def ph_projA():
        p = mkprog()
        wq_rdy = (sG, sG.n)
        t_rdy = (sV, sV.n)
        ev_q = proj_fm(p, wA, act_bf,
                       lambda oc, g: qT[:, oc, g * 512:(g + 1) * 512],
                       wq_rdy, t_rdy, [], bias=bqc)
        ev_k = proj_fm(p, wB, act_bf,
                       lambda oc, g: kT[:, oc, g * 512:(g + 1) * 512],
                       wq_rdy, t_rdy, ev_q[-4:], bias=bkc)
        av_tok = p.dma("gpsimd", lambda e: e.dma_start(
            wA[:], avT.ap().rearrange("(c p) o -> p c o", p=128)),
            waits=[ev_q[-1]], inc=sG)
        ev_v = proj_tm_v(p, wA, act_bf, av_tok, t_rdy, ev_k[-4:])
        ao_tok = p.dma("gpsimd", lambda e: e.dma_start(
            wB[:], aoT.ap().rearrange("(c p) o -> p c o", p=128)),
            waits=[ev_k[-1]], inc=sG)
        ev_bop = {}
        for oc in range(DC):
            bank = pb[4 + oc % 2]
            w0 = [ao_tok, (sG, sG.n)] if oc == 0 else []
            if oc >= 2:
                w0.append(ev_bop[oc - 2])
            last = None
            for cf in range(DC):
                last = p.op("tensor", lambda e, oc=oc, cf=cf, bank=bank:
                            e.matmul(
                    bank[:, 0:1], wB[:, cf, oc * 128:(oc + 1) * 128],
                    bvc_b[:, cf:cf + 1],
                    start=(cf == 0), stop=(cf == DC - 1)),
                    waits=(w0 if cf == 0 else []), inc=sP)
            ev_bop[oc] = p.op("vector", lambda e, oc=oc, bank=bank:
                              e.tensor_copy(bop[:, oc:oc + 1],
                                            bank[:, 0:1]),
                              waits=[last], inc=sV)
        p.op("vector", lambda e: e.tensor_tensor(
            bop[:], bop[:], boc[:], ALU.add),
            waits=[ev_bop[DC - 1]], inc=sV)
        dump_map["qT"] = (qT, [128, DC, S], BF16)
        dump_map["kT"] = (kT, [128, DC, S], BF16)
        dump_map["v_sb"] = (v_sb, [128, NT, NH, 65], BF16)
        return p

    def ph_cbound(which):
        negc = negcA if which == 0 else negcB
        p = mkprog()
        grp_guard = (sV, sV.n)
        sq_rd = {0: None, 1: None}
        for src, acc in ((qT, q16), (kT, k16)):
            for g in range(NG):
                last = None
                for c in range(DC):
                    w = [grp_guard] if c == 0 else []
                    if sq_rd[c % 2]:
                        w.append(sq_rd[c % 2])
                    tsq = p.op("scalar", lambda e, c=c, g=g, src=src:
                               e.activation(
                        sqb[:, c % 2, :], src[:, c, g * 512:(g + 1) * 512],
                        AF.Square), waits=w, inc=sS)
                    last = p.op("tensor", lambda e, c=c: e.matmul(
                        pb[0][0:16, :], ind[:, c, :], sqb[:, c % 2, :],
                        start=(c == 0), stop=(c == DC - 1)),
                        waits=[tsq], inc=sP)
                    sq_rd[c % 2] = last
                if g == 0 and src is qT or g == 0 and src is kT:
                    grp_guard = p.op("vector", lambda e, acc=acc:
                                     e.reduce_max(acc[:], pb[0][0:16, :],
                                                  axis=AXX),
                                     waits=[last], inc=sV)
                else:
                    p.op("vector", lambda e: e.reduce_max(
                        t16[:], pb[0][0:16, :], axis=AXX),
                        waits=[last], inc=sV)
                    grp_guard = p.op("vector", lambda e, acc=acc:
                                     e.tensor_tensor(acc[:], acc[:], t16[:],
                                                     ALU.max), inc=sV)
        p.op("vector", lambda e: e.tensor_tensor(
            t16[:], q16[:], k16[:], ALU.mult), waits=[grp_guard], inc=sV)
        tq = p.op("scalar", lambda e: e.activation(t16[:], t16[:], AF.Sqrt),
                  waits=[(sV, sV.n)], inc=sS)
        tn = p.op("vector", lambda e: e.tensor_scalar(
            negc16[:], t16[:], -0.12758, None, ALU.mult), waits=[tq], inc=sV)
        td = p.dma("sync", lambda e: e.dma_start(
            nscr.ap()[which, :], negc16[:]), waits=[tn], inc=sA)
        tl = p.dma("sync", lambda e: e.dma_start(
            ncrow[:], nscr.ap()[which:which + 1, :]), waits=[td], inc=sA)
        tb = p.op("tensor", lambda e: e.matmul(
            pb[2][:, 0:16], ones_f[:], ncrow[:], start=True, stop=True),
            waits=[tl], inc=sP)
        p.op("vector", lambda e, negc=negc: e.tensor_copy(
            negc[:], pb[2][:, 0:16]), waits=[tb], inc=sV)
        negc3 = negc[:].rearrange("p (a two) -> p a two", two=2)
        negcP = negcPB if which else negcPA
        p.op("vector", lambda e, negc3=negc3, negcP=negcP: e.tensor_tensor(
            negcP[:], negc3[:, :, 0], negc3[:, :, 1], ALU.min), inc=sV)
        dump_map["negc"] = (negc, [128, 16], F32)
        return p

    def ph_attn(causal):
        p = mkprog()
        negc = negcB if causal else negcA
        negcP = negcPB if causal else negcPA
        mask_rdy = None
        if causal:
            for j in range(4):
                p.op("gpsimd", lambda e, j=j: e.memset(masks[:, j, :], 1.0))
                mask_rdy = p.op("gpsimd", lambda e, j=j: e.affine_select(
                    masks[:, j, :], masks[:, j, :], [[1, 512]], ALU.is_ge,
                    0.0, base=-128 * j, channel_multiplier=-1), inc=sGc)
        prev_evac = [(sV, sV.n), (sV, sV.n)]
        pb6_guard = (sV, sV.n)
        for hp in range(NH // 2):
            for g in range(NG):
                chunks = list(range(4 * (g + 1))) if causal else list(range(NT))
                nck = len(chunks)
                exp_toks, ctx_toks, sc_toks = {}, {}, {}

                def emit_scores(ci):
                    c = chunks[ci]
                    for par in (0, 1):
                        bank = pb[2 * (ci % 2) + par]
                        w = [exp_toks[(ci - 2, par)]] if ci >= 2 \
                            else [prev_evac[par]]
                        sc_toks[(ci, par)] = p.op(
                            "tensor", lambda e, c=c, g=g, hp=hp, par=par,
                            bank=bank: e.matmul(
                                bank[:],
                                kT[64 * par:64 * par + 64, hp,
                                   c * 128:(c + 1) * 128],
                                qT[64 * par:64 * par + 64, hp,
                                   g * 512:(g + 1) * 512],
                                start=True, stop=True), waits=w, inc=sP)
                    pbW = pbW0 if ci % 2 == 0 else pbW1
                    w = [sc_toks[(ci, 0)], sc_toks[(ci, 1)]]
                    if ci >= 4:
                        w += [ctx_toks[(ci - 4, 0)], ctx_toks[(ci - 4, 1)]]
                    tok = p.op("scalar", lambda e, ci=ci, hp=hp, pbW=pbW:
                               e.activation(
                        P_sb[:, ci % 4, :, :].rearrange("p a b -> p (a b)"),
                        pbW[:], AF.Exp,
                        bias=negcP[:, hp:hp + 1], scale=0.125),
                        waits=w, inc=sS)
                    for par in (0, 1):
                        if causal and c >= 4 * g:
                            j = c - 4 * g
                            exp_toks[(ci, par)] = p.op(
                                "vector", lambda e, ci=ci, par=par, j=j:
                                e.tensor_tensor(
                                    P_sb[:, ci % 4, par, :],
                                    P_sb[:, ci % 4, par, :], masks[:, j, :],
                                    ALU.mult), waits=[tok, mask_rdy], inc=sV)
                        else:
                            exp_toks[(ci, par)] = tok

                def emit_ctx(ci):
                    c = chunks[ci]
                    for par in (0, 1):
                        h = 2 * hp + par
                        w = [exp_toks[(ci, par)]]
                        if ci == 0:
                            w += prev_evac
                        ctx_toks[(ci, par)] = p.op(
                            "tensor", lambda e, c=c, ci=ci, par=par, h=h,
                            nck=nck:
                            e.matmul(pb[4 + par][0:65, :],
                                     v_sb[:, c, h, :],
                                     P_sb[:, ci % 4, par, :],
                                     start=(ci == 0), stop=(ci == nck - 1)),
                            waits=w, inc=sP)

                for ci in range(nck):
                    emit_scores(ci)
                    if ci >= 2:
                        emit_ctx(ci - 2)
                emit_ctx(nck - 2)
                emit_ctx(nck - 1)
                tv_ = p.op("vector", lambda e: e.reciprocal(
                    vrowW[:], pbW2[64:65, :]),
                    waits=[ctx_toks[(nck - 1, 0)], ctx_toks[(nck - 1, 1)]],
                    inc=sV)
                for par in (0, 1):
                    h = 2 * hp + par
                    tb = p.op("tensor", lambda e, par=par: e.matmul(
                        pb[6][0:64, :], ones_f[0:1, 0:64],
                        vrowW[0:1, 512 * par:512 * par + 512],
                        start=True, stop=True),
                        waits=[tv_, pb6_guard], inc=sP)
                    pb6_guard = p.op("vector", lambda e: e.tensor_copy(
                        rb_sb[:], pb[6][0:64, :]), waits=[tb], inc=sV)
                    prev_evac[par] = p.op(
                        "vector", lambda e, par=par, hp=hp, g=g:
                        e.tensor_tensor(
                            ctxT[64 * par:64 * par + 64, hp,
                                 g * 512:(g + 1) * 512],
                            pb[4 + par][0:64, :], rb_sb[:], ALU.mult),
                        inc=sV)
        dump_map["ctxT"] = (ctxT, [128, DC, S], BF16)
        return p

    def ph_woA():
        p = mkprog()
        wo_rdy = (sG, sG.n)
        ctx_rdy = (sV, sV.n)
        ev = proj_fm(p, wB, ctxT,
                     lambda oc, g: ao_sb[:, oc, g * 512:(g + 1) * 512],
                     wo_rdy, ctx_rdy, [], bias=bop)
        dump_map["ao_sb"] = (ao_sb, [128, DC, S], BF16)
        sq_guard = [None, None]
        row_guard = [None, None]
        xs_rd = {}
        out_hist = []
        tO_rd = {}
        for g in range(NG):
            def src(c, g=g):
                return ao_sb[:, c, g * 512:(g + 1) * 512], None
            st, _ = ln_stats(p, src, True, pb[4], pb[5],
                             [ev[-1]] if g == 0 else [], sq_guard)
            rt = ln_rows(p, pb[4], pb[5], st, eps5c, row_guard)
            tb_m = bcast(p, pb[6], mrow[:], [rt])
            tb_r = bcast(p, pb[7], r2row[:], [rt])
            row_guard = [tb_m, tb_r]
            for c in range(DC):
                buf = c % 2
                w = [xs_rd.get(buf)] if xs_rd.get(buf) else []
                tok = p.dma("sync", lambda e, c=c, g=g, buf=buf: e.dma_start(
                    xs[:, buf, :], xT.ap()[c * 128:(c + 1) * 128,
                                           g * 512:(g + 1) * 512]),
                    waits=w, inc=sA)
                t1 = p.op("vector", lambda e, c=c, g=g: e.tensor_tensor(
                    tmp1[:], ao_sb[:, c, g * 512:(g + 1) * 512], pb[6][:],
                    ALU.subtract), waits=[tb_m, tb_r], inc=sV)
                t2 = p.op("vector", lambda e: e.tensor_tensor(
                    tmp2[:], tmp1[:], pb[7][:], ALU.mult), inc=sV)
                wO = [tO_rd.get(buf)] if tO_rd.get(buf) else []
                t3 = p.op("vector", lambda e, buf=buf: e.tensor_tensor(
                    tmpO[:, buf, :], tmp2[:], xs[:, buf, :], ALU.add),
                    waits=[tok] + wO, inc=sV)
                xs_rd[buf] = t3
                ot = p.dma("sync", lambda e, c=c, g=g, buf=buf: e.dma_start(
                    x1d.ap()[c, :, g * 512:(g + 1) * 512], tmpO[:, buf, :]),
                    waits=[t3], inc=sA)
                tO_rd[buf] = ot
                out_hist.append(ot)
        return p

    def ph_ln3():
        p = mkprog()
        x1_rdy = (sA, sA.n)
        sq_guard = [None, None]
        row_guard = [None, None]
        xs_rd = {}
        tO_rd = {0: [], 1: []}
        rms_last = {}
        xs_rd[0] = [x1_rdy]
        xs_rd[1] = [x1_rdy]
        for g in range(NG):
            st, _ = ln_stats(
                p, None, False, pb[0], pb[1], [], sq_guard,
                dram_fn=lambda c, g=g: x1d.ap()[c, :,
                                                g * 512:(g + 1) * 512],
                g=g, xs_state=xs_rd)
            rt = ln_rows(p, pb[0], pb[1], st, eps5c, row_guard)
            tb_m = bcast(p, pb[6], mrow[:], [rt])
            tb_r = bcast(p, pb[7], r2row[:], [rt])
            row_guard = [tb_m, tb_r]
            for c in range(DC):
                buf = c % 2
                tok = p.dma("sync", lambda e, c=c, g=g, buf=buf: e.dma_start(
                    xs[:, buf, :], x1d.ap()[c, :, g * 512:(g + 1) * 512]),
                    waits=xs_rd.get(buf, []), inc=sA)
                t1 = p.op("vector", lambda e, buf=buf: e.tensor_tensor(
                    tmp1[:], xs[:, buf, :], pb[6][:], ALU.subtract),
                    waits=[tok, tb_m, tb_r], inc=sV)
                xs_rd[buf] = [t1]
                t2 = p.op("vector", lambda e, buf=buf: e.tensor_tensor(
                    tmpO[:, buf, :], tmp1[:], pb[7][:], ALU.mult),
                    waits=tO_rd[buf], inc=sV)
                ot = p.dma("sync", lambda e, c=c, g=g, buf=buf: e.dma_start(
                    yd.ap()[c, :, g * 512:(g + 1) * 512], tmpO[:, buf, :]),
                    waits=[t2], inc=sA)
                tsq = p.op("scalar", lambda e, buf=buf: e.activation(
                    sqb[:, buf, :], tmpO[:, buf, :], AF.Square),
                    waits=[t2] + ([sq_guard[buf]] if sq_guard[buf] else []),
                    inc=sS)
                mt = p.op("tensor", lambda e, c=c, g=g, buf=buf: e.matmul(
                    pb[2 + g][0:1, :], ones_cb[:], sqb[:, buf, :],
                    start=(c == 0), stop=(c == DC - 1)), waits=[tsq], inc=sP)
                sq_guard[buf] = mt
                rms_last[g] = mt
                tO_rd[buf] = [ot, tsq]
        y_written = (sA, sA.n)
        ap_guard = {0: [], 1: []}
        for g in range(NG):
            ts = p.op("scalar", lambda e, g=g: e.activation(
                r2row[:], pb[2 + g][0:1, :], AF.Sqrt, bias=eps6c[0:1, :],
                scale=1.0 / D), waits=[rms_last[g]] + row_guard, inc=sS)
            tr = p.op("vector", lambda e: e.reciprocal(r2row[:], r2row[:]),
                      waits=[ts], inc=sV)
            tb = bcast(p, pb[6], r2row[:], [tr] + row_guard)
            row_guard = [tb]
            for c in range(DC):
                buf = c % 2
                w = list(ap_guard[buf])
                if g == 0 and c < 2:
                    w.append(y_written)
                    w += xs_rd.get(buf, [])
                tok = p.dma("sync", lambda e, c=c, g=g, buf=buf: e.dma_start(
                    xs[:, buf, :], yd.ap()[c, :, g * 512:(g + 1) * 512]),
                    waits=w, inc=sA)
                at = p.op("vector", lambda e, c=c, g=g, buf=buf:
                          e.scalar_tensor_tensor(
                    act_bf[:, c, g * 512:(g + 1) * 512], xs[:, buf, :],
                    anwc[:, c:c + 1], pb[6][:], ALU.mult, ALU.mult),
                    waits=[tok, tb], inc=sV)
                ap_guard[buf] = [at]
        dump_map["t2"] = (act_bf, [128, DC, S], BF16)
        return p

    def ph_projB():
        p = mkprog()
        l1 = p.dma("gpsimd", lambda e: e.dma_start(
            wA[:], wqT.ap().rearrange("(c p) o -> p c o", p=128)), inc=sG)
        l2 = p.dma("gpsimd", lambda e: e.dma_start(
            wB[:], wkT.ap().rearrange("(c p) o -> p c o", p=128)), inc=sG)
        t_rdy = (sV, sV.n)
        ev_q = proj_fm(p, wA, act_bf,
                       lambda oc, g: qT[:, oc, g * 512:(g + 1) * 512],
                       l1, t_rdy, [])
        ev_k = proj_fm(p, wB, act_bf,
                       lambda oc, g: kT[:, oc, g * 512:(g + 1) * 512],
                       l2, t_rdy, ev_q[-4:])
        lv = p.dma("gpsimd", lambda e: e.dma_start(
            wA[:], wvT.ap().rearrange("(c p) o -> p c o", p=128)),
            waits=[ev_q[-1]], inc=sG)
        proj_tm_v(p, wA, act_bf, lv, t_rdy, ev_k[-4:])
        return p

    def ph_rope():
        p = mkprog()
        t0 = p.op("gpsimd", lambda e: e.iota(
            invf[:], [[0, 4], [1, 32]], channel_multiplier=0, base=0,
            allow_small_or_imprecise_dtypes=True), inc=sGc)
        t2 = p.op("vector", lambda e: e.tensor_scalar(
            invf[:], invf[:], -math.log(10000.0) / 32.0, None,
            ALU.mult), waits=[t0], inc=sV)
        t3 = p.op("scalar", lambda e: e.activation(invf[:], invf[:], AF.Exp),
                  waits=[t2], inc=sS)
        def range_reduce(first_waits, src_psum):
            c8 = 2 * PI * 256.0
            t_m = p.op("vector", lambda e: e.tensor_scalar(
                tmp2[:], src_psum, c8, None, ALU.is_ge),
                waits=first_waits, inc=sV)
            p.op("vector", lambda e: e.tensor_scalar(
                tmp2[:], tmp2[:], c8, None, ALU.mult), inc=sV)
            p.op("vector", lambda e: e.tensor_tensor(
                tmp1[:], src_psum, tmp2[:], ALU.subtract), inc=sV)
            for j in range(7, -1, -1):
                cj = 2 * PI * (2.0 ** j)
                p.op("vector", lambda e, cj=cj: e.tensor_scalar(
                    tmp2[:], tmp1[:], cj, None, ALU.is_ge), inc=sV)
                p.op("vector", lambda e, cj=cj: e.tensor_scalar(
                    tmp2[:], tmp2[:], cj, None, ALU.mult), inc=sV)
                tl = p.op("vector", lambda e: e.tensor_tensor(
                    tmp1[:], tmp1[:], tmp2[:], ALU.subtract), inc=sV)
            return tl

        prev_te = None
        prev_cos = None
        for g in range(NG):
            ti = p.op("gpsimd", lambda e, g=g: e.iota(
                vrow[:], [[1, 512]], base=g * 512, channel_multiplier=0,
                allow_small_or_imprecise_dtypes=True),
                waits=([prev_cos] if prev_cos else []), inc=sGc)
            tf = p.op("tensor", lambda e: e.matmul(
                pb[0][:], invf[:], vrow[:], start=True, stop=True),
                waits=[ti, t3] + ([prev_cos] if prev_cos else []), inc=sP)
            ta = range_reduce([tf] + ([prev_te] if prev_te else []), pb[0][:])
            tb = p.op("scalar", lambda e: e.activation(
                tmp2[:], tmp1[:], AF.Sin, bias=npic[:]), waits=[ta], inc=sS)
            tc = p.op("vector", lambda e, g=g: e.scalar_tensor_tensor(
                sinT[:, g * 512:(g + 1) * 512], tmp2[:], nsgn[:], tmp2[:],
                ALU.mult, ALU.bypass), waits=[tb], inc=sV)
            td0 = p.op("vector", lambda e: e.tensor_scalar(
                tmp1[:], tmp1[:], PI / 2, None, ALU.add),
                waits=[tb], inc=sV)
            p.op("vector", lambda e: e.tensor_scalar(
                tmp2[:], tmp1[:], 2 * PI, None, ALU.is_ge), inc=sV)
            p.op("vector", lambda e: e.tensor_scalar(
                tmp2[:], tmp2[:], 2 * PI, None, ALU.mult), inc=sV)
            td = p.op("vector", lambda e: e.tensor_tensor(
                tmp1[:], tmp1[:], tmp2[:], ALU.subtract), inc=sV)
            te = p.op("scalar", lambda e: e.activation(
                tmp2[:], tmp1[:], AF.Sin, bias=npic[:]), waits=[td, tc],
                inc=sS)
            prev_te = te
            prev_cos = p.op("vector", lambda e, g=g: e.tensor_scalar(
                cosT[:, g * 512:(g + 1) * 512], tmp2[:], -1.0, None,
                ALU.mult), waits=[te], inc=sV)
        tab_rdy = prev_cos
        last_ta = None
        for src in (qT, kT):
            for c in range(DC):
                rd = None
                for (d0, s0) in ((0, 32), (32, 0), (64, 96), (96, 64)):
                    rd = p.dma("sync", lambda e, c=c, d0=d0, s0=s0, src=src:
                               e.dma_start(rot[d0:d0 + 32, :],
                                           src[s0:s0 + 32, c, :]),
                               waits=([last_ta] if (last_ta and d0 == 0)
                                      else []), inc=sA)
                for g in range(NG):
                    ta = p.op("vector", lambda e, g=g: e.tensor_tensor(
                        rpA[:], rot[:, g * 512:(g + 1) * 512],
                        sinT[:, g * 512:(g + 1) * 512], ALU.mult),
                        waits=[rd, tab_rdy], inc=sV)
                    last_ta = ta
                    p.op("vector", lambda e, c=c, g=g, src=src:
                         e.tensor_tensor(
                        rpB[:], src[:, c, g * 512:(g + 1) * 512],
                        cosT[:, g * 512:(g + 1) * 512], ALU.mult), inc=sV)
                    p.op("vector", lambda e, c=c, g=g, src=src:
                         e.tensor_tensor(
                        src[:, c, g * 512:(g + 1) * 512], rpA[:], rpB[:],
                        ALU.add), inc=sV)
        dump_map["ropeq"] = (qT, [128, DC, S], BF16)
        dump_map["cosT"] = (cosT, [128, S], F32)
        dump_map["sinT"] = (sinT, [128, S], F32)
        return p

    def ph_woB():
        p = mkprog()
        y_rdy = (sA, sA.n)
        lo = p.dma("gpsimd", lambda e: e.dma_start(
            wA[:], woT.ap().rearrange("(c p) o -> p c o", p=128)), inc=sG)
        ctx_rdy = (sV, sV.n)
        evac = []
        xs_rd = {}
        gi = 0
        banks = [pb[0], pb[1], pb[2], pb[3]]
        for oc in range(DC):
            for g in range(NG):
                bank = banks[gi % 4]
                waits = [lo, ctx_rdy]
                if gi >= 4:
                    waits.append(evac[gi - 4])
                for ic in range(DC):
                    tok = p.op("tensor", lambda e, ic=ic, oc=oc, g=g,
                               bank=bank: e.matmul(
                        bank[:], wA[:, ic, oc * 128:(oc + 1) * 128],
                        ctxT[:, ic, g * 512:(g + 1) * 512],
                        start=(ic == 0), stop=(ic == DC - 1)),
                        waits=(waits if ic == 0 else []), inc=sP)
                buf = gi % 2
                w2_ = [xs_rd.get(buf)] if xs_rd.get(buf) else []
                if gi < 2:
                    w2_.append(y_rdy)
                yt = p.dma("sync", lambda e, oc=oc, g=g, buf=buf: e.dma_start(
                    xs[:, buf, :], yd.ap()[oc, :, g * 512:(g + 1) * 512]),
                    waits=w2_, inc=sA)
                et = p.op("vector", lambda e, oc=oc, g=g, bank=bank, buf=buf:
                          e.tensor_tensor(
                    h_sb[:, oc, g * 512:(g + 1) * 512], bank[:],
                    xs[:, buf, :], ALU.add), waits=[tok, yt], inc=sV)
                xs_rd[buf] = et
                evac.append(et)
                gi += 1
        dump_map["h_sb"] = (h_sb, [128, DC, S], F32)
        return p

    def ph_gate():
        p = mkprog()
        h_rdy = (sV, sV.n)
        sq_rd = {0: None, 1: None}
        rms_tok = {}
        for g in range(NG):
            last = None
            for c in range(DC):
                w = [h_rdy] if (g == 0 and c == 0) else []
                if sq_rd[c % 2]:
                    w.append(sq_rd[c % 2])
                tsq = p.op("scalar", lambda e, c=c, g=g: e.activation(
                    sqb[:, c % 2, :], h_sb[:, c, g * 512:(g + 1) * 512],
                    AF.Square), waits=w, inc=sS)
                last = p.op("tensor", lambda e, c=c, g=g: e.matmul(
                    pb[2 + g][0:1, :], ones_cb[:], sqb[:, c % 2, :],
                    start=(c == 0), stop=(c == DC - 1)), waits=[tsq], inc=sP)
                sq_rd[c % 2] = last
            rms_tok[g] = last
        row_guard = None
        ap_last = None
        for g in range(NG):
            ts = p.op("scalar", lambda e, g=g: e.activation(
                r2row[:], pb[2 + g][0:1, :], AF.Sqrt, bias=eps6c[0:1, :],
                scale=1.0 / D),
                waits=[rms_tok[g]] + ([row_guard] if row_guard else []),
                inc=sS)
            tr = p.op("vector", lambda e: e.reciprocal(r2row[:], r2row[:]),
                      waits=[ts], inc=sV)
            tb = bcast(p, pb[6], r2row[:],
                       [tr] + ([ap_last] if ap_last else []))
            row_guard = tb
            for c in range(DC):
                ap_last = p.op("vector", lambda e, c=c, g=g:
                               e.scalar_tensor_tensor(
                    act_bf[:, c, g * 512:(g + 1) * 512],
                    h_sb[:, c, g * 512:(g + 1) * 512],
                    mnwc[:, c:c + 1], pb[6][:], ALU.mult, ALU.mult),
                    waits=[tb], inc=sV)
        tn_rdy = ap_last
        dump_map["tn"] = (act_bf, [128, DC, S], BF16)
        ev = []
        for ti in range(NT):
            bank = pb[ti % 2]
            w = [tn_rdy]
            if ti >= 2:
                w.append(ev[ti - 2])
            for ic in range(DC):
                tok = p.op("tensor", lambda e, ti=ti, ic=ic, bank=bank:
                           e.matmul(
                    bank[:, 0:8], act_bf[:, ic, ti * 128:(ti + 1) * 128],
                    gw_b[:, ic, :], start=(ic == 0), stop=(ic == DC - 1)),
                    waits=(w if ic == 0 else []), inc=sP)
            et = p.op("vector", lambda e, ti=ti, bank=bank: e.tensor_copy(
                logits[:, ti, :], bank[:, 0:8]), waits=[tok], inc=sV)
            ev.append(et)
        t2_prev = None
        for ti in range(NT):
            tmx = p.op("vector", lambda e, ti=ti: e.reduce_max(
                rw1[:], logits[:, ti, :], axis=AXX),
                waits=[ev[ti]] + ([t2_prev] if t2_prev else []), inc=sV)
            p.op("vector", lambda e, ti=ti: e.scalar_tensor_tensor(
                n1s[:], logits[:, ti, :], rw1[:], logits[:, ti, :],
                ALU.subtract, ALU.bypass), waits=[tmx], inc=sV)
            p.op("vector", lambda e: e.tensor_scalar(
                n1s[:], n1s[:], 0.0, None, ALU.is_ge), inc=sV)
            p.op("vector", lambda e: e.tensor_scalar(
                scr8[:], n1s[:], 1e30, None, ALU.mult), inc=sV)
            p.op("vector", lambda e, ti=ti: e.tensor_tensor(
                L2s[:], logits[:, ti, :], scr8[:], ALU.subtract), inc=sV)
            tm2 = p.op("vector", lambda e: e.reduce_max(
                rw2[:], L2s[:], axis=AXX), inc=sV)
            p.op("vector", lambda e: e.scalar_tensor_tensor(
                n2s[:], L2s[:], rw2[:], L2s[:], ALU.subtract, ALU.bypass),
                waits=[tm2], inc=sV)
            p.op("vector", lambda e: e.tensor_scalar(
                n2s[:], n2s[:], 0.0, None, ALU.is_ge), inc=sV)
            td = p.op("vector", lambda e: e.tensor_tensor(
                d12[:], rw1[:], rw2[:], ALU.subtract), inc=sV)
            p.op("scalar", lambda e: e.activation(
                rw1[:], d12[:], AF.Sigmoid), waits=[td], inc=sS)
            t2 = p.op("scalar", lambda e: e.activation(
                rw2[:], d12[:], AF.Sigmoid, scale=-1.0), inc=sS)
            t2_prev = t2
            t3 = p.op("vector", lambda e: e.scalar_tensor_tensor(
                w2s[:], n1s[:], rw1[:], n1s[:], ALU.mult, ALU.bypass),
                waits=[t2], inc=sV)
            p.op("vector", lambda e, ti=ti: e.scalar_tensor_tensor(
                wtm[:, ti, :], n2s[:], rw2[:], w2s[:], ALU.mult, ALU.add),
                waits=[t3], inc=sV)
        wt = (sV, sV.n)
        tz = p.op("vector", lambda e: e.memset(w32a[:], 0.0), waits=[wt],
                  inc=sV)
        tc_ = p.op("vector", lambda e: e.tensor_copy(
            w32a3[:, :, 0:8], wtm[:]), waits=[tz], inc=sV)
        tt_ = p.op("vector", lambda e: e.transpose(w32b[:], w32a[:]),
                   waits=[tc_], inc=sV)
        wem3 = wem[:].rearrange("e (tt a i) -> e tt a i", a=4, i=32)
        for a in range(4):
            p.dma("gpsimd", lambda e, a=a: e.dma_start(
                wem3[:, :, a, :], w32b3[32 * a:32 * a + 8, :, :]),
                waits=[tt_], inc=sG)
        p.dma("gpsimd", lambda e: e.dma_start(b2b[:], b2_d.ap()), inc=sG)
        dump_map["wtm"] = (wtm, [128, NT, 8], F32)
        return p

    def ph_moe():
        p = mkprog()
        wem_rdy = (sG, sG.n)
        tn_rdy = (sV, sV.n)
        pre = []
        gi = 0
        for dc in range(DC):
            for g in range(NG):
                bank = pb[gi % 2]
                w = [wem_rdy]
                if gi >= 2:
                    w.append(pre[gi - 2])
                tok = p.op("tensor", lambda e, dc=dc, g=g, bank=bank:
                           e.matmul(
                    bank[:], b2b[:, dc * 128:(dc + 1) * 128],
                    wem[:, g * 512:(g + 1) * 512], start=True, stop=True),
                    waits=w, inc=sP)
                et = p.op("vector", lambda e, dc=dc, g=g, bank=bank:
                          e.tensor_tensor(
                    h_sb[:, dc, g * 512:(g + 1) * 512], bank[:],
                    h_sb[:, dc, g * 512:(g + 1) * 512], ALU.add),
                    waits=[tok], inc=sV)
                pre.append(et)
                gi += 1
        prev_acc = pre[-1]
        blk_end = []
        gelu_bank_guard = {0: None, 1: None, 2: None, 3: None}
        tmpg_guard = {0: None, 1: None, 2: None, 3: None}
        hid_guard = {}
        mm2_evac = []
        acc_toks = []
        acc_toks_by = {"pool": [], "dve": []}
        prev_wb = []
        for e_i in range(E):
            wrow_tok = p.dma("sync", lambda e, e_i=e_i: e.dma_start(
                wrow[:], wem[e_i:e_i + 1, :]),
                waits=[wem_rdy] + prev_wb, inc=sA)
            wb_toks = {}
            for g in range(NG):
                mmt = p.op("tensor", lambda e, g=g: e.matmul(
                    pb[4 + g][:], ones_b[:],
                    wrow[:, g * 512:(g + 1) * 512],
                    start=True, stop=True),
                    waits=[wrow_tok, prev_acc], inc=sP)
                wb_toks[g] = p.op("vector", lambda e, g=g: e.tensor_copy(
                    wb_sb[:, g, :], pb[4 + g][:]), waits=[mmt], inc=sV)
            prev_wb = list(wb_toks.values())
            for q8 in range(8):
                j = e_i * 8 + q8
                buf = q8 % 2
                wl = [blk_end[j - 2]] if j >= 2 else []
                lt1 = p.dma("gpsimd", lambda e, e_i=e_i, q8=q8, buf=buf:
                            e.dma_start(
                    w1c[:, buf, :, :],
                    w1T.ap()[e_i, :, q8 * 512:(q8 + 1) * 512].rearrange(
                        "(c p) h -> p c h", p=128)), waits=wl, inc=sG)
                lt2 = p.dma("gpsimd", lambda e, e_i=e_i, q8=q8, buf=buf:
                            e.dma_start(
                    w2c[:, buf, :, :],
                    w2T.ap()[e_i, q8 * 512:(q8 + 1) * 512, :].rearrange(
                        "(c p) d -> p c d", p=128)), inc=sG)
                last_mm2 = None
                gelu_toks = {}

                def emit_mm1(g, j=j, buf=buf, lt1=lt1, e_i=e_i, q8=q8):
                    gb = g % 2
                    toks = []
                    for hc in range(4):
                        bank = pb[[0, 1, 4, 5][hc]]
                        w = [lt1]
                        if j == 0 and g == 0 and hc == 0:
                            w += [tn_rdy, prev_acc]
                        if q8 == 0 and g == 0 and hc >= 2:
                            w.append(wb_toks[hc - 2])
                        if gelu_bank_guard[hc]:
                            w.append(gelu_bank_guard[hc])
                        for ic in range(DC):
                            tok = p.op("tensor", lambda e, hc=hc, ic=ic, g=g,
                                       buf=buf, bank=bank: e.matmul(
                                bank[:],
                                w1c[:, buf, ic, hc * 128:(hc + 1) * 128],
                                act_bf[:, ic, g * 512:(g + 1) * 512],
                                start=(ic == 0), stop=(ic == DC - 1)),
                                waits=(w if ic == 0 else []), inc=sP)
                        hw_ = [tok]
                        if tmpg_guard[hc]:
                            hw_.append(tmpg_guard[hc])
                        ht = p.op("scalar", lambda e, hc=hc,
                                  e_i=e_i, q8=q8, bank=bank: e.activation(
                            tmpG[:, hc, :], bank[:], AF.Gelu,
                            bias=b1c[:, e_i, q8 * 4 + hc:q8 * 4 + hc + 1]),
                            waits=hw_, inc=sS)
                        gelu_bank_guard[hc] = ht
                        sw_ = [ht, wb_toks[g]]
                        if hid_guard.get((gb, hc)):
                            sw_.append(hid_guard[(gb, hc)])
                        st_ = p.op("vector", lambda e, hc=hc, gb=gb, g=g:
                                   e.tensor_tensor(
                            hid[:, gb, hc, :], tmpG[:, hc, :],
                            wb_sb[:, g, :], ALU.mult), waits=sw_, inc=sV)
                        tmpg_guard[hc] = st_
                        toks.append(st_)
                    gelu_toks[g] = toks

                def emit_mm2(g, buf=buf, lt2=lt2, q8=q8):
                    nonlocal last_mm2
                    gb = g % 2
                    for dc in range(DC):
                        bank = pb[[2, 3, 6, 7][dc % 4]]
                        w = [lt2] + gelu_toks[g]
                        if q8 == 0 and g == 0 and dc % 4 >= 2:
                            w.append(wb_toks[dc % 4])
                        if len(mm2_evac) >= 4:
                            w.append(mm2_evac[-4])
                        for hc in range(4):
                            tok = p.op("tensor", lambda e, dc=dc, hc=hc,
                                       gb=gb, buf=buf, bank=bank: e.matmul(
                                bank[:],
                                w2c[:, buf, hc, dc * 128:(dc + 1) * 128],
                                hid[:, gb, hc, :],
                                start=(hc == 0), stop=(hc == 3)),
                                waits=(w if hc == 0 else []), inc=sP)
                        last_mm2 = tok
                        for hc in range(4):
                            hid_guard[(gb, hc)] = tok
                        t2 = p.op("vector", lambda e, dc=dc, g=g, bank=bank:
                                  e.tensor_tensor(
                            h_sb[:, dc, g * 512:(g + 1) * 512], bank[:],
                            h_sb[:, dc, g * 512:(g + 1) * 512], ALU.add),
                            waits=[tok], inc=sV)
                        mm2_evac.append(t2)
                        acc_toks.append(t2)

                emit_mm1(0)
                for g in range(NG):
                    if g + 1 < NG:
                        emit_mm1(g + 1)
                    emit_mm2(g)
                blk_end.append(last_mm2)
            prev_acc = acc_toks[-1]
        dump_map["moe_h"] = (h_sb, [128, DC, S], F32)
        return p

    def ph_final():
        p = mkprog()
        h_rdy = (sGc, sGc.n)
        h_rdy2 = (sV, sV.n)
        x1_rdy = (sA, sA.n)
        sq_guard = [None, None]
        row_guard = [None, None]
        xs_rd = {}
        tO_rd = {}
        last_out = None
        for g in range(NG):
            def src(c, g=g):
                return h_sb[:, c, g * 512:(g + 1) * 512], None
            st, _ = ln_stats(p, src, False, pb[0], pb[1],
                             [h_rdy, h_rdy2] if g == 0 else [], sq_guard)
            rt = ln_rows(p, pb[0], pb[1], st, eps5c, row_guard)
            tb_m = bcast(p, pb[6], mrow[:], [rt])
            tb_r = bcast(p, pb[7], r2row[:], [rt])
            row_guard = [tb_m, tb_r]
            for c in range(DC):
                buf = c % 2
                w = [xs_rd.get(buf)] if xs_rd.get(buf) else []
                if g == 0 and c < 2:
                    w.append(x1_rdy)
                tok = p.dma("sync", lambda e, c=c, g=g, buf=buf: e.dma_start(
                    xs[:, buf, :], x1d.ap()[c, :, g * 512:(g + 1) * 512]),
                    waits=w, inc=sA)
                t1 = p.op("vector", lambda e, c=c, g=g: e.tensor_tensor(
                    tmp1[:], h_sb[:, c, g * 512:(g + 1) * 512], pb[6][:],
                    ALU.subtract), waits=[tb_m, tb_r], inc=sV)
                t2 = p.op("vector", lambda e: e.tensor_tensor(
                    tmp2[:], tmp1[:], pb[7][:], ALU.mult), inc=sV)
                wO = [tO_rd.get(buf)] if tO_rd.get(buf) else []
                t3 = p.op("vector", lambda e, buf=buf: e.tensor_tensor(
                    tmpO[:, buf, :], tmp2[:], xs[:, buf, :], ALU.add),
                    waits=[tok] + wO, inc=sV)
                xs_rd[buf] = t3
                ot = p.dma("sync", lambda e, c=c, g=g, buf=buf: e.dma_start(
                    outT.ap()[c * 128:(c + 1) * 128,
                              g * 512:(g + 1) * 512], tmpO[:, buf, :]),
                    waits=[t3], inc=sO)
                tO_rd[buf] = ot
                last_out = ot
        p.op("sync", lambda e: e.nop(), waits=[last_out])
        return p

    phases = [
        ("init", ph_init),
        ("ln1", ph_ln1),
        ("projA", ph_projA),
        ("cboundA", lambda: ph_cbound(0)),
        ("attnA", lambda: ph_attn(False)),
        ("woA", ph_woA),
        ("ln3", ph_ln3),
        ("projB", ph_projB),
        ("rope", ph_rope),
        ("cboundB", lambda: ph_cbound(1)),
        ("attnB", lambda: ph_attn(True)),
        ("woB", ph_woB),
        ("gate", ph_gate),
        ("moe", ph_moe),
        ("final", ph_final),
    ]
    for name, fn in phases:
        prog = fn()
        prog.emit(nc)
        if stop_after is not None and name == stop_after:
            break

    if dump is not None:
        ap_t, shp, dt = dump_map[dump]
        flat = int(np.prod(shp[1:]))
        dbg = nc.dram_tensor("dbg", [shp[0], flat], dt, kind="ExternalOutput")
        pd = mkprog()
        src_ap = ap_t[:] if len(shp) == 2 else ap_t[:].rearrange(
            "p ... -> p (...)")
        pd.dma("sync", lambda e: e.dma_start(dbg.ap(), src_ap), inc=sO)
        pd.op("sync", lambda e: e.nop(), waits=[(sO, sO.n)])
        pd.emit(nc)

    nc.finalize()
    return nc


def _host_pack(inputs):
    g = {k: np.asarray(v) for k, v in inputs.items()}
    asc = np.ascontiguousarray

    def colpack(v):
        return asc(v.reshape(8, 128).T)
    common = {
        "aqT": asc(g["L0_attn_Wq"].T), "akT": asc(g["L0_attn_Wk"].T),
        "avT": asc(g["L0_attn_Wv"].T), "aoT": asc(g["L0_attn_Wo"].T),
        "bqc": colpack(g["L0_attn_bq"]), "bkc": colpack(g["L0_attn_bk"]),
        "bvc": colpack(g["L0_attn_bv"]), "boc": colpack(g["L0_attn_bo"]),
        "anwc": colpack(g["L0_ffn_attn_norm_w"]),
        "mnwc": colpack(g["L0_ffn_moe_norm_w"]),
        "wqT": asc(g["L0_ffn_wq"].T), "wkT": asc(g["L0_ffn_wk"].T),
        "wvT": asc(g["L0_ffn_wv"].T), "woT": asc(g["L0_ffn_wo"].T),
        "gwT": asc(g["L0_ffn_gate_w"].T),
        "w1T": asc(g["L0_ffn_w1"].transpose(0, 2, 1)),
        "w2T": asc(g["L0_ffn_w2"].transpose(0, 2, 1)),
        "b1c": asc(g["L0_ffn_b1"].reshape(E, 32, 128).transpose(2, 0, 1)),
        "b2": asc(g["L0_ffn_b2"]),
    }
    x = g["L0_attn_inputs"]
    in_maps = []
    for b in range(8):
        m = dict(common)
        m["xT"] = asc(x[b].T)
        in_maps.append(m)
    return in_maps


def kernel(**inputs):
    in_maps = _host_pack(inputs)
    nc = build_program()
    res = run_bass_kernel_spmd(nc, in_maps, core_ids=list(range(8)))
    out = np.stack([np.ascontiguousarray(np.asarray(r["outT"]).T)
                    for r in res.results], axis=0)
    return out.astype(np.float32)


# version 13 (final)
# speedup vs baseline: 1.0180x; 1.0180x over previous
import math
import numpy as np
import concourse.bass as bass
import concourse.mybir as mybir
from concourse.bass_utils import run_bass_kernel_spmd

F32 = mybir.dt.float32
BF16 = mybir.dt.bfloat16
AF = mybir.ActivationFunctionType
ALU = mybir.AluOpType
AXX = mybir.AxisListType.X

S = 2048
D = 1024
NH = 16
E = 8
HH = 4096
DC = 8
NG = 4
NT = 16
PI = math.pi


class Sem:
    def __init__(self, nc, name):
        self.h = nc.alloc_semaphore(name)
        self.n = 0


class Prog:
    ENGINES = ("sync", "gpsimd", "tensor", "scalar", "vector")

    def __init__(self, chain_sems=None):
        self.steps = {e: [] for e in self.ENGINES}
        self.chain_sems = chain_sems or {}
        self.last_tok = {}

    def op(self, eng, fn, waits=(), inc=None, amt=1):
        waits = [t for t in waits if t is not None]
        cs = self.chain_sems.get(eng)
        if cs is not None:
            if eng in self.last_tok:
                waits = waits + [self.last_tok[eng]]
            if inc is None:
                inc = cs
                amt = 1
        w = [(s.h, v) for (s, v) in waits if s is not None]
        incs = []
        if inc is not None:
            inc.n += amt
            incs.append((inc.h, amt))
        self.steps[eng].append((fn, w, incs))
        tok = (inc, inc.n) if inc is not None else None
        if cs is not None and tok is not None and tok[0] is cs:
            self.last_tok[eng] = tok
        return tok

    def dma(self, eng, fn, waits=(), inc=None):
        return self.op(eng, fn, waits, inc, amt=16)

    def emit(self, nc):
        with nc.Block() as blk:
            for ename in self.ENGINES:
                steps = self.steps[ename]
                if not steps:
                    continue

                def mk(steps):
                    def f(eng):
                        for fn, waits, incs in steps:
                            for sem, val in waits:
                                eng.wait_ge(sem, val)
                            inst = fn(eng)
                            for sem, amt in incs:
                                inst.then_inc(sem, amt)
                    return f
                getattr(blk, ename)(mk(steps))


def build_program(stop_after=None, dump=None):
    nc = bass.Bass("TRN2", target_bir_lowering=False,
                   detect_race_conditions=False)

    xT = nc.dram_tensor("xT", [D, S], F32, kind="ExternalInput")
    aqT = nc.dram_tensor("aqT", [D, D], F32, kind="ExternalInput")
    akT = nc.dram_tensor("akT", [D, D], F32, kind="ExternalInput")
    avT = nc.dram_tensor("avT", [D, D], F32, kind="ExternalInput")
    aoT = nc.dram_tensor("aoT", [D, D], F32, kind="ExternalInput")
    bqc_d = nc.dram_tensor("bqc", [128, 8], F32, kind="ExternalInput")
    bkc_d = nc.dram_tensor("bkc", [128, 8], F32, kind="ExternalInput")
    bvc_d = nc.dram_tensor("bvc", [128, 8], F32, kind="ExternalInput")
    boc_d = nc.dram_tensor("boc", [128, 8], F32, kind="ExternalInput")
    anw_d = nc.dram_tensor("anwc", [128, 8], F32, kind="ExternalInput")
    mnw_d = nc.dram_tensor("mnwc", [128, 8], F32, kind="ExternalInput")
    wqT = nc.dram_tensor("wqT", [D, D], F32, kind="ExternalInput")
    wkT = nc.dram_tensor("wkT", [D, D], F32, kind="ExternalInput")
    wvT = nc.dram_tensor("wvT", [D, D], F32, kind="ExternalInput")
    woT = nc.dram_tensor("woT", [D, D], F32, kind="ExternalInput")
    gwT = nc.dram_tensor("gwT", [D, E], F32, kind="ExternalInput")
    w1T = nc.dram_tensor("w1T", [E, D, HH], F32, kind="ExternalInput")
    w2T = nc.dram_tensor("w2T", [E, HH, D], F32, kind="ExternalInput")
    b1c_d = nc.dram_tensor("b1c", [128, E, 32], F32, kind="ExternalInput")
    b2_d = nc.dram_tensor("b2", [E, D], F32, kind="ExternalInput")

    outT = nc.dram_tensor("outT", [D, S], F32, kind="ExternalOutput")
    x1d = nc.dram_tensor("x1d", [DC, 128, S], F32, kind="Internal")
    yd = nc.dram_tensor("yd", [DC, 128, S], F32, kind="Internal")
    wscr = nc.dram_tensor("wscr", [S, E], F32, kind="Internal")
    nscr = nc.dram_tensor("nscr", [2, 16], F32, kind="Internal")

    A = nc.alloc_sbuf_tensor
    ones_f = A("ones_f", [1, 128], F32)
    ones_b = A("ones_b", [1, 128], BF16)
    ones_c = A("ones_c", [128, 1], F32)
    ones_cb = A("ones_cb", [128, 1], BF16)
    bqc = A("bqc_s", [128, 8], F32)
    bkc = A("bkc_s", [128, 8], F32)
    bvc_b = A("bvc_b", [128, 8], BF16)
    boc = A("boc_s", [128, 8], F32)
    anwc = A("anwc_s", [128, 8], F32)
    mnwc = A("mnwc_s", [128, 8], F32)
    bop = A("bop", [128, 8], F32)
    b1c = A("b1c_s", [128, E, 32], F32)
    gw_b = A("gw_b", [128, 8, E], BF16)
    ind = A("ind", [128, DC, 16], BF16)
    negcA = A("negcA", [128, 16], F32)
    negcB = A("negcB", [128, 16], F32)
    negcPA = A("negcPA", [128, 8], F32)
    negcPB = A("negcPB", [128, 8], F32)
    negc16 = A("negc16", [16, 1], F32)
    q16 = A("q16", [16, 1], F32)
    k16 = A("k16", [16, 1], F32)
    t16 = A("t16", [16, 1], F32)
    ncrow = A("ncrow", [1, 16], F32)
    mrow = A("mrow", [1, 512], F32)
    r2row = A("r2row", [1, 512], F32)
    r3row = A("r3row", [1, 512], F32)
    vrow = A("vrow", [1, 512], F32)
    vrow2 = A("vrow2", [1, 512], F32)
    rb2_sb = A("rb2_sb", [64, 512], F32)
    invf = A("invf", [1, 128], F32)
    nsgn = A("nsgn", [128, 1], F32)
    eps5c = A("eps5c", [128, 1], F32)
    eps6c = A("eps6c", [128, 1], F32)
    npic = A("npic", [128, 1], F32)
    max8 = A("max8", [128, 8], F32)
    scr8 = A("scr8", [128, 8], F32)
    L2s = A("L2s", [128, 8], F32)
    L3s = A("L3s", [128, 8], F32)
    n1s = A("n1s", [128, 8], F32)
    n2s = A("n2s", [128, 8], F32)
    w2s = A("w2s", [128, 8], F32)
    ones8f = A("ones8f", [128, 8], F32)
    d12 = A("d12", [128, 1], F32)
    rw1 = A("rw1", [128, 1], F32)
    rw2 = A("rw2", [128, 1], F32)
    logits = A("logits", [128, NT, 8], F32)
    tmpG = A("tmpG", [128, 4, 512], BF16)
    wtm = A("wtm", [128, NT, 8], F32)

    base0 = (nc.sbuf_base + 63) & ~63
    KB = 1024

    def AT(name, shape, dtype, off):
        return nc.alloc_sbuf_tensor_at(name, shape, dtype, offset=base0 + off)

    act_bf = AT("act_bf", [128, DC, S], BF16, 0)
    ctxT = AT("ctxT", [128, DC, S], BF16, 0)
    wA = AT("wA", [128, DC, D], BF16, 32 * KB)
    hid = AT("hid", [128, 2, 4, 512], BF16, 32 * KB)
    wB = AT("wB", [128, DC, D], BF16, 48 * KB)
    cosT = AT("cosT", [128, S], F32, 48 * KB)
    sinT = AT("sinT", [128, S], F32, 56 * KB)
    wem = AT("wem", [E, S], BF16, 48 * KB)
    b2b = AT("b2b", [E, D], BF16, 52 * KB)
    wrow = AT("wrow", [1, S], BF16, 54 * KB)
    wb_sb = AT("wb_sb", [128, 4, 512], BF16, 58 * KB)
    qT = AT("qT", [128, DC, S], BF16, 64 * KB)
    ao_sb = AT("ao_sb", [128, DC, S], BF16, 64 * KB)
    kT = AT("kT", [128, DC, S], BF16, 96 * KB)
    h_sb = AT("h_sb", [128, DC, S], F32, 64 * KB)
    v_sb = AT("v_sb", [128, NT, NH, 65], BF16, 128 * KB)
    w1c = AT("w1c", [128, 2, DC, 512], BF16, 128 * KB)
    w2c = AT("w2c", [128, 2, 4, D], BF16, 144 * KB)
    SC = 166 * KB
    xs = AT("xs", [128, 2, 512], F32, SC)
    sqb = AT("sqb", [128, 2, 512], BF16, SC + 4096)
    tmp1 = AT("tmp1", [128, 512], F32, SC + 6144)
    tmp2 = AT("tmp2", [128, 512], F32, SC + 8192)
    tmpO = AT("tmpO", [128, 2, 512], F32, SC + 10240)
    P_sb = AT("P_sb", [128, 4, 2, 512], BF16, SC)
    masks = AT("masks", [128, 4, 512], BF16, SC + 10240)
    rb_sb = AT("rb_sb", [64, 512], F32, SC + 8192)
    w32a = AT("w32a", [128, 512], F32, SC + 6144)
    w32b = AT("w32b", [128, 512], F32, SC + 8192)
    w32a3 = w32a[:].rearrange("p (t i) -> p t i", i=32)
    w32b3 = w32b[:].rearrange("p (t i) -> p t i", i=32)
    rot = AT("rot", [128, S], BF16, SC)
    rpA = AT("rpA", [128, 512], BF16, SC + 4096)
    rpB = AT("rpB", [128, 512], BF16, SC + 5120)
    assert base0 + SC + 14336 <= nc.sbuf_top, (base0, nc.sbuf_top)

    pbW0 = nc.alloc_psum_tensor("pbW0", [128, 1024], F32)
    pbW1 = nc.alloc_psum_tensor("pbW1", [128, 1024], F32)
    pb = [pbW0[:, 0:512], pbW0[:, 512:1024],
          pbW1[:, 0:512], pbW1[:, 512:1024]] + [
        nc.alloc_psum_tensor(f"pb{i}", [128, 512], F32) for i in range(4, 8)]

    sA = Sem(nc, "sA")
    sG = Sem(nc, "sG")
    sGc = Sem(nc, "sGc")
    sP = Sem(nc, "sP")
    sS = Sem(nc, "sS")
    sV = Sem(nc, "sV")
    sO = Sem(nc, "sO")

    dump_map = {}

    def mkprog():
        return Prog({"vector": sV, "gpsimd": sGc})

    class St:
        pass

    def ln_stats(p, src_fn, src_bf, mean_bank, ssq_bank, first_waits,
                 sq_guard, dram_fn=None, g=0, xs_state=None):
        mm2_hist = []
        local_toks = {}
        for c in range(DC):
            if dram_fn is not None:
                buf = c % 2
                w = list(xs_state.get(buf, []))
                if c >= 2:
                    w += local_toks[c - 2]
                tok = p.dma("sync", lambda e, c=c, buf=buf: e.dma_start(
                    xs[:, buf, :], dram_fn(c)), waits=w, inc=sA)
                ap, rdy = xs[:, buf, :], tok
            else:
                ap, rdy = src_fn(c)
            w_sq = []
            if rdy:
                w_sq.append(rdy)
            if c == 0:
                w_sq += first_waits
            if c >= 2:
                w_sq.append(mm2_hist[c - 2])
            elif sq_guard[c % 2] is not None:
                w_sq.append(sq_guard[c % 2])
            t_sq = p.op("scalar", lambda e, ap=ap, c=c: e.activation(
                sqb[:, c % 2, :], ap, AF.Square), waits=w_sq, inc=sS)
            w_mm = ([rdy] if rdy else []) + (first_waits if c == 0 else [])
            lhs = ones_cb if src_bf else ones_c
            t1 = p.op("tensor", lambda e, ap=ap, c=c, lhs=lhs: e.matmul(
                mean_bank[0:1, :], lhs[:], ap,
                start=(c == 0), stop=(c == DC - 1)), waits=w_mm, inc=sP)
            t2 = p.op("tensor", lambda e, c=c: e.matmul(
                ssq_bank[0:1, :], ones_cb[:], sqb[:, c % 2, :],
                start=(c == 0), stop=(c == DC - 1)), waits=[t_sq], inc=sP)
            mm2_hist.append(t2)
            local_toks[c] = [t1, t_sq]
            if dram_fn is not None and xs_state is not None:
                xs_state[c % 2] = [t1, t_sq]
            sq_guard[c % 2] = t2
        return mm2_hist[-1], local_toks

    def ln_rows(p, mean_bank, ssq_bank, stats_tok, eps_ap, row_guard):
        w0 = [stats_tok] + [t for t in row_guard if t]
        p.op("vector", lambda e: e.tensor_scalar(
            mrow[:], mean_bank[0:1, :], 1.0 / D, None, ALU.mult),
            waits=w0, inc=sV)
        p.op("vector", lambda e: e.tensor_scalar(
            r2row[:], ssq_bank[0:1, :], 1.0 / D, None, ALU.mult), inc=sV)
        p.op("vector", lambda e: e.tensor_tensor(
            r3row[:], mrow[:], mrow[:], ALU.mult), inc=sV)
        tv = p.op("vector", lambda e: e.tensor_tensor(
            r2row[:], r2row[:], r3row[:], ALU.subtract), inc=sV)
        ts = p.op("scalar", lambda e: e.activation(
            r2row[:], r2row[:], AF.Sqrt, bias=eps_ap[0:1, :]),
            waits=[tv], inc=sS)
        return p.op("vector", lambda e: e.reciprocal(r2row[:], r2row[:]),
                    waits=[ts], inc=sV)

    def bcast(p, bank, row_ap, waits, m=128):
        return p.op("tensor", lambda e, bank=bank, row_ap=row_ap: e.matmul(
            bank[0:m, :], ones_f[0:1, 0:m], row_ap, start=True, stop=True),
            waits=waits, inc=sP)

    def proj_fm(p, w_sb, act, out_cb, w_ready, act_tok, prior, bias=None):
        banks = [pb[0], pb[1], pb[2], pb[3]]
        evac = []
        gi = 0
        for oc in range(DC):
            for g in range(NG):
                bank = banks[gi % 4]
                waits = [w_ready, act_tok]
                waits.append(evac[gi - 4] if gi >= 4 else
                             (prior[gi % 4] if gi < len(prior) else None))
                for ic in range(DC):
                    tok = p.op("tensor", lambda e, ic=ic, oc=oc, g=g,
                               bank=bank: e.matmul(
                        bank[:], w_sb[:, ic, oc * 128:(oc + 1) * 128],
                        act[:, ic, g * 512:(g + 1) * 512],
                        start=(ic == 0), stop=(ic == DC - 1)),
                        waits=(waits if ic == 0 else []), inc=sP)
                dst = out_cb(oc, g)
                if bias is not None:
                    et = p.op("scalar", lambda e, dst=dst, bank=bank, oc=oc:
                              e.activation(dst, bank[:], AF.Identity,
                                           bias=bias[:, oc:oc + 1]),
                              waits=[tok], inc=sS)
                else:
                    et = p.op("scalar", lambda e, dst=dst, bank=bank:
                              e.activation(dst, bank[:], AF.Copy),
                              waits=[tok], inc=sS)
                evac.append(et)
                gi += 1
        return evac

    def proj_tm_v(p, w_sb, act, w_ready, act_tok, prior):
        evac = []
        gi = 0
        banks = [pb[0], pb[1], pb[2], pb[3]]
        for tt in range(NT):
            for dvg in range(2):
                bank = banks[gi % 4]
                waits = [w_ready, act_tok]
                waits.append(evac[gi - 4] if gi >= 4 else
                             (prior[gi % 4] if gi < len(prior) else None))
                for ic in range(DC):
                    tok = p.op("tensor", lambda e, ic=ic, tt=tt, dvg=dvg,
                               bank=bank: e.matmul(
                        bank[:], act[:, ic, tt * 128:(tt + 1) * 128],
                        w_sb[:, ic, dvg * 512:(dvg + 1) * 512],
                        start=(ic == 0), stop=(ic == DC - 1)),
                        waits=(waits if ic == 0 else []), inc=sP)
                et = p.op("vector", lambda e, tt=tt, dvg=dvg, bank=bank:
                          e.tensor_copy(
                    v_sb[:, tt, dvg * 8:(dvg + 1) * 8, 0:64],
                    bank[:].rearrange("p (h d) -> p h d", d=64)),
                    waits=[tok], inc=sV)
                evac.append(et)
                gi += 1
        return evac

    def ph_init():
        p = mkprog()
        for dst, src in ((bqc, bqc_d), (bkc, bkc_d), (boc, boc_d),
                         (anwc, anw_d), (mnwc, mnw_d)):
            p.dma("sync", lambda e, d=dst, s=src: e.dma_start(d[:], s.ap()),
                  inc=sA)
        p.dma("sync", lambda e: e.dma_start(b1c[:], b1c_d.ap()), inc=sA)
        p.dma("gpsimd", lambda e: e.dma_start(
            wA[:], aqT.ap().rearrange("(c p) o -> p c o", p=128)), inc=sG)
        p.dma("gpsimd", lambda e: e.dma_start(
            wB[:], akT.ap().rearrange("(c p) o -> p c o", p=128)), inc=sG)
        p.dma("gpsimd", lambda e: e.dma_start(bvc_b[:], bvc_d.ap()), inc=sG)
        p.dma("gpsimd", lambda e: e.dma_start(
            gw_b[:], gwT.ap().rearrange("(c p) e -> p c e", p=128)), inc=sG)
        p.op("gpsimd", lambda e: e.memset(ones_f[:], 1.0))
        p.op("gpsimd", lambda e: e.memset(ones_b[:], 1.0))
        p.op("gpsimd", lambda e: e.memset(ones_c[:], 1.0))
        p.op("gpsimd", lambda e: e.memset(ones_cb[:], 1.0))
        p.op("gpsimd", lambda e: e.memset(ones8f[:], 1.0))
        p.op("gpsimd", lambda e: e.memset(ind[:], 0.0))
        for c in range(DC):
            p.op("gpsimd", lambda e, c=c: e.memset(
                ind[0:64, c, 2 * c:2 * c + 1], 1.0))
            p.op("gpsimd", lambda e, c=c: e.memset(
                ind[64:128, c, 2 * c + 1:2 * c + 2], 1.0))
        p.op("gpsimd", lambda e: e.memset(v_sb[:, :, :, 64:65], 1.0))
        p.op("gpsimd", lambda e: e.memset(eps5c[:], 1e-5))
        p.op("gpsimd", lambda e: e.memset(eps6c[:], 1e-6))
        p.op("gpsimd", lambda e: e.memset(npic[:], -PI))
        p.op("gpsimd", lambda e: e.memset(nsgn[0:32, :], 1.0))
        p.op("gpsimd", lambda e: e.memset(nsgn[32:64, :], -1.0))
        p.op("gpsimd", lambda e: e.memset(nsgn[64:96, :], 1.0))
        p.op("gpsimd", lambda e: e.memset(nsgn[96:128, :], -1.0), inc=sGc)
        return p

    def ph_ln1():
        p = mkprog()
        sq_guard = [None, None]
        row_guard = [None, None]
        xs_rd = {}
        apply_guard = {0: None, 1: None}
        for g in range(NG):
            st, _ = ln_stats(
                p, None, False, pb[0], pb[1], [], sq_guard,
                dram_fn=lambda c, g=g: xT.ap()[c * 128:(c + 1) * 128,
                                               g * 512:(g + 1) * 512],
                g=g, xs_state=xs_rd)
            rt = ln_rows(p, pb[0], pb[1], st, eps5c, row_guard)
            tb_m = bcast(p, pb[6], mrow[:], [rt, apply_guard[0]])
            tb_r = bcast(p, pb[7], r2row[:], [rt, apply_guard[1]])
            row_guard = [tb_m, tb_r]
            last_t2 = None
            for c in range(DC):
                buf = c % 2
                tok = p.dma("sync", lambda e, c=c, g=g, buf=buf: e.dma_start(
                    xs[:, buf, :], xT.ap()[c * 128:(c + 1) * 128,
                                           g * 512:(g + 1) * 512]),
                    waits=xs_rd.get(buf, []), inc=sA)
                t1 = p.op("vector", lambda e, buf=buf: e.tensor_tensor(
                    tmp1[:], xs[:, buf, :], pb[6][:], ALU.subtract),
                    waits=[tok, tb_m, tb_r], inc=sV)
                xs_rd[buf] = [t1]
                last_t2 = p.op("vector", lambda e, c=c, g=g: e.tensor_tensor(
                    act_bf[:, c, g * 512:(g + 1) * 512], tmp1[:], pb[7][:],
                    ALU.mult), inc=sV)
            apply_guard = {0: last_t2, 1: last_t2}
        dump_map["t_bf"] = (act_bf, [128, DC, S], BF16)
        return p

    def ph_projA():
        p = mkprog()
        wq_rdy = (sG, sG.n)
        t_rdy = (sV, sV.n)
        ev_q = proj_fm(p, wA, act_bf,
                       lambda oc, g: qT[:, oc, g * 512:(g + 1) * 512],
                       wq_rdy, t_rdy, [], bias=bqc)
        ev_k = proj_fm(p, wB, act_bf,
                       lambda oc, g: kT[:, oc, g * 512:(g + 1) * 512],
                       wq_rdy, t_rdy, ev_q[-4:], bias=bkc)
        av_tok = p.dma("gpsimd", lambda e: e.dma_start(
            wA[:], avT.ap().rearrange("(c p) o -> p c o", p=128)),
            waits=[ev_q[-1]], inc=sG)
        ev_v = proj_tm_v(p, wA, act_bf, av_tok, t_rdy, ev_k[-4:])
        ao_tok = p.dma("gpsimd", lambda e: e.dma_start(
            wB[:], aoT.ap().rearrange("(c p) o -> p c o", p=128)),
            waits=[ev_k[-1]], inc=sG)
        ev_bop = {}
        for oc in range(DC):
            bank = pb[4 + oc % 2]
            w0 = [ao_tok, (sG, sG.n)] if oc == 0 else []
            if oc >= 2:
                w0.append(ev_bop[oc - 2])
            last = None
            for cf in range(DC):
                last = p.op("tensor", lambda e, oc=oc, cf=cf, bank=bank:
                            e.matmul(
                    bank[:, 0:1], wB[:, cf, oc * 128:(oc + 1) * 128],
                    bvc_b[:, cf:cf + 1],
                    start=(cf == 0), stop=(cf == DC - 1)),
                    waits=(w0 if cf == 0 else []), inc=sP)
            ev_bop[oc] = p.op("vector", lambda e, oc=oc, bank=bank:
                              e.tensor_copy(bop[:, oc:oc + 1],
                                            bank[:, 0:1]),
                              waits=[last], inc=sV)
        p.op("vector", lambda e: e.tensor_tensor(
            bop[:], bop[:], boc[:], ALU.add),
            waits=[ev_bop[DC - 1]], inc=sV)
        dump_map["qT"] = (qT, [128, DC, S], BF16)
        dump_map["kT"] = (kT, [128, DC, S], BF16)
        dump_map["v_sb"] = (v_sb, [128, NT, NH, 65], BF16)
        return p

    def ph_cbound(which):
        negc = negcA if which == 0 else negcB
        p = mkprog()
        grp_guard = (sV, sV.n)
        sq_rd = {0: None, 1: None}
        for src, acc in ((qT, q16), (kT, k16)):
            for g in range(NG):
                last = None
                for c in range(DC):
                    w = [grp_guard] if c == 0 else []
                    if sq_rd[c % 2]:
                        w.append(sq_rd[c % 2])
                    tsq = p.op("scalar", lambda e, c=c, g=g, src=src:
                               e.activation(
                        sqb[:, c % 2, :], src[:, c, g * 512:(g + 1) * 512],
                        AF.Square), waits=w, inc=sS)
                    last = p.op("tensor", lambda e, c=c: e.matmul(
                        pb[0][0:16, :], ind[:, c, :], sqb[:, c % 2, :],
                        start=(c == 0), stop=(c == DC - 1)),
                        waits=[tsq], inc=sP)
                    sq_rd[c % 2] = last
                if g == 0 and src is qT or g == 0 and src is kT:
                    grp_guard = p.op("vector", lambda e, acc=acc:
                                     e.reduce_max(acc[:], pb[0][0:16, :],
                                                  axis=AXX),
                                     waits=[last], inc=sV)
                else:
                    p.op("vector", lambda e: e.reduce_max(
                        t16[:], pb[0][0:16, :], axis=AXX),
                        waits=[last], inc=sV)
                    grp_guard = p.op("vector", lambda e, acc=acc:
                                     e.tensor_tensor(acc[:], acc[:], t16[:],
                                                     ALU.max), inc=sV)
        p.op("vector", lambda e: e.tensor_tensor(
            t16[:], q16[:], k16[:], ALU.mult), waits=[grp_guard], inc=sV)
        tq = p.op("scalar", lambda e: e.activation(t16[:], t16[:], AF.Sqrt),
                  waits=[(sV, sV.n)], inc=sS)
        tn = p.op("vector", lambda e: e.tensor_scalar(
            negc16[:], t16[:], -0.12758, None, ALU.mult), waits=[tq], inc=sV)
        td = p.dma("sync", lambda e: e.dma_start(
            nscr.ap()[which, :], negc16[:]), waits=[tn], inc=sA)
        tl = p.dma("sync", lambda e: e.dma_start(
            ncrow[:], nscr.ap()[which:which + 1, :]), waits=[td], inc=sA)
        tb = p.op("tensor", lambda e: e.matmul(
            pb[2][:, 0:16], ones_f[:], ncrow[:], start=True, stop=True),
            waits=[tl], inc=sP)
        p.op("vector", lambda e, negc=negc: e.tensor_copy(
            negc[:], pb[2][:, 0:16]), waits=[tb], inc=sV)
        negc3 = negc[:].rearrange("p (a two) -> p a two", two=2)
        negcP = negcPB if which else negcPA
        p.op("vector", lambda e, negc3=negc3, negcP=negcP: e.tensor_tensor(
            negcP[:], negc3[:, :, 0], negc3[:, :, 1], ALU.min), inc=sV)
        dump_map["negc"] = (negc, [128, 16], F32)
        return p

    def ph_attn(causal):
        p = mkprog()
        negc = negcB if causal else negcA
        negcP = negcPB if causal else negcPA
        mask_rdy = None
        if causal:
            for j in range(4):
                p.op("gpsimd", lambda e, j=j: e.memset(masks[:, j, :], 1.0))
                mask_rdy = p.op("gpsimd", lambda e, j=j: e.affine_select(
                    masks[:, j, :], masks[:, j, :], [[1, 512]], ALU.is_ge,
                    0.0, base=-128 * j, channel_multiplier=-1), inc=sGc)
        prev_evac = [(sV, sV.n), (sV, sV.n)]
        tail_exps = {0: [(sV, sV.n)], 1: [(sV, sV.n)]}
        pb6_guard = (sV, sV.n)
        pb7_guard = (sV, sV.n)
        for hp in range(NH // 2):
            for g in range(NG):
                chunks = list(range(4 * (g + 1))) if causal else list(range(NT))
                nck = len(chunks)
                exp_toks, ctx_toks, sc_toks = {}, {}, {}

                def emit_scores(ci):
                    c = chunks[ci]
                    for par in (0, 1):
                        bank = pb[2 * (ci % 2) + par]
                        w = [exp_toks[(ci - 2, par)]] if ci >= 2 \
                            else list(tail_exps[ci % 2])
                        sc_toks[(ci, par)] = p.op(
                            "tensor", lambda e, c=c, g=g, hp=hp, par=par,
                            bank=bank: e.matmul(
                                bank[:],
                                kT[64 * par:64 * par + 64, hp,
                                   c * 128:(c + 1) * 128],
                                qT[64 * par:64 * par + 64, hp,
                                   g * 512:(g + 1) * 512],
                                start=True, stop=True), waits=w, inc=sP)
                    pbW = pbW0 if ci % 2 == 0 else pbW1
                    w = [sc_toks[(ci, 0)], sc_toks[(ci, 1)]]
                    if ci >= 4:
                        w += [ctx_toks[(ci - 4, 0)], ctx_toks[(ci - 4, 1)]]
                    tok = p.op("scalar", lambda e, ci=ci, hp=hp, pbW=pbW:
                               e.activation(
                        P_sb[:, ci % 4, :, :].rearrange("p a b -> p (a b)"),
                        pbW[:], AF.Exp,
                        bias=negcP[:, hp:hp + 1], scale=0.125),
                        waits=w, inc=sS)
                    for par in (0, 1):
                        if causal and c >= 4 * g:
                            j = c - 4 * g
                            exp_toks[(ci, par)] = p.op(
                                "vector", lambda e, ci=ci, par=par, j=j:
                                e.tensor_tensor(
                                    P_sb[:, ci % 4, par, :],
                                    P_sb[:, ci % 4, par, :], masks[:, j, :],
                                    ALU.mult), waits=[tok, mask_rdy], inc=sV)
                        else:
                            exp_toks[(ci, par)] = tok

                def emit_ctx(ci):
                    c = chunks[ci]
                    for par in (0, 1):
                        h = 2 * hp + par
                        w = [exp_toks[(ci, par)]]
                        if ci == 0:
                            w.append(prev_evac[par])
                        ctx_toks[(ci, par)] = p.op(
                            "tensor", lambda e, c=c, ci=ci, par=par, h=h,
                            nck=nck:
                            e.matmul(pb[4 + par][0:65, :],
                                     v_sb[:, c, h, :],
                                     P_sb[:, ci % 4, par, :],
                                     start=(ci == 0), stop=(ci == nck - 1)),
                            waits=w, inc=sP)

                for ci in range(nck):
                    emit_scores(ci)
                    if ci >= 2:
                        emit_ctx(ci - 2)
                emit_ctx(nck - 2)
                emit_ctx(nck - 1)
                tail_exps = {0: [exp_toks[(nck - 2, 0)],
                                 exp_toks[(nck - 2, 1)]],
                             1: [exp_toks[(nck - 1, 0)],
                                 exp_toks[(nck - 1, 1)]]}
                for par in (0, 1):
                    h = 2 * hp + par
                    vr = vrow if par == 0 else vrow2
                    rb = rb_sb if par == 0 else rb2_sb
                    bb = pb[6] if par == 0 else pb[7]
                    bguard = pb6_guard if par == 0 else pb7_guard
                    tv_ = p.op("vector", lambda e, par=par, vr=vr:
                               e.reciprocal(
                        vr[:], pb[4 + par][64:65, :]),
                        waits=[ctx_toks[(nck - 1, par)]], inc=sV)
                    tb = p.op("tensor", lambda e, vr=vr, bb=bb: e.matmul(
                        bb[0:64, :], ones_f[0:1, 0:64], vr[:],
                        start=True, stop=True),
                        waits=[tv_, bguard], inc=sP)
                    bc = p.op("vector", lambda e, rb=rb, bb=bb: e.tensor_copy(
                        rb[:], bb[0:64, :]), waits=[tb], inc=sV)
                    if par == 0:
                        pb6_guard = bc
                    else:
                        pb7_guard = bc
                    prev_evac[par] = p.op(
                        "vector", lambda e, par=par, hp=hp, g=g, rb=rb:
                        e.tensor_tensor(
                            ctxT[64 * par:64 * par + 64, hp,
                                 g * 512:(g + 1) * 512],
                            pb[4 + par][0:64, :], rb[:], ALU.mult),
                        inc=sV)
        dump_map["ctxT"] = (ctxT, [128, DC, S], BF16)
        return p

    def ph_woA():
        p = mkprog()
        wo_rdy = (sG, sG.n)
        ctx_rdy = (sV, sV.n)
        ev = proj_fm(p, wB, ctxT,
                     lambda oc, g: ao_sb[:, oc, g * 512:(g + 1) * 512],
                     wo_rdy, ctx_rdy, [], bias=bop)
        dump_map["ao_sb"] = (ao_sb, [128, DC, S], BF16)
        sq_guard = [None, None]
        row_guard = [None, None]
        xs_rd = {}
        out_hist = []
        tO_rd = {}
        for g in range(NG):
            def src(c, g=g):
                return ao_sb[:, c, g * 512:(g + 1) * 512], None
            st, _ = ln_stats(p, src, True, pb[4], pb[5],
                             [ev[-1]] if g == 0 else [], sq_guard)
            rt = ln_rows(p, pb[4], pb[5], st, eps5c, row_guard)
            tb_m = bcast(p, pb[6], mrow[:], [rt])
            tb_r = bcast(p, pb[7], r2row[:], [rt])
            row_guard = [tb_m, tb_r]
            for c in range(DC):
                buf = c % 2
                w = [xs_rd.get(buf)] if xs_rd.get(buf) else []
                tok = p.dma("sync", lambda e, c=c, g=g, buf=buf: e.dma_start(
                    xs[:, buf, :], xT.ap()[c * 128:(c + 1) * 128,
                                           g * 512:(g + 1) * 512]),
                    waits=w, inc=sA)
                t1 = p.op("vector", lambda e, c=c, g=g: e.tensor_tensor(
                    tmp1[:], ao_sb[:, c, g * 512:(g + 1) * 512], pb[6][:],
                    ALU.subtract), waits=[tb_m, tb_r], inc=sV)
                t2 = p.op("vector", lambda e: e.tensor_tensor(
                    tmp2[:], tmp1[:], pb[7][:], ALU.mult), inc=sV)
                wO = [tO_rd.get(buf)] if tO_rd.get(buf) else []
                t3 = p.op("vector", lambda e, buf=buf: e.tensor_tensor(
                    tmpO[:, buf, :], tmp2[:], xs[:, buf, :], ALU.add),
                    waits=[tok] + wO, inc=sV)
                xs_rd[buf] = t3
                ot = p.dma("sync", lambda e, c=c, g=g, buf=buf: e.dma_start(
                    x1d.ap()[c, :, g * 512:(g + 1) * 512], tmpO[:, buf, :]),
                    waits=[t3], inc=sA)
                tO_rd[buf] = ot
                out_hist.append(ot)
        return p

    def ph_ln3():
        p = mkprog()
        x1_rdy = (sA, sA.n)
        sq_guard = [None, None]
        row_guard = [None, None]
        xs_rd = {}
        tO_rd = {0: [], 1: []}
        rms_last = {}
        xs_rd[0] = [x1_rdy]
        xs_rd[1] = [x1_rdy]
        for g in range(NG):
            st, _ = ln_stats(
                p, None, False, pb[0], pb[1], [], sq_guard,
                dram_fn=lambda c, g=g: x1d.ap()[c, :,
                                                g * 512:(g + 1) * 512],
                g=g, xs_state=xs_rd)
            rt = ln_rows(p, pb[0], pb[1], st, eps5c, row_guard)
            tb_m = bcast(p, pb[6], mrow[:], [rt])
            tb_r = bcast(p, pb[7], r2row[:], [rt])
            row_guard = [tb_m, tb_r]
            for c in range(DC):
                buf = c % 2
                tok = p.dma("sync", lambda e, c=c, g=g, buf=buf: e.dma_start(
                    xs[:, buf, :], x1d.ap()[c, :, g * 512:(g + 1) * 512]),
                    waits=xs_rd.get(buf, []), inc=sA)
                t1 = p.op("vector", lambda e, buf=buf: e.tensor_tensor(
                    tmp1[:], xs[:, buf, :], pb[6][:], ALU.subtract),
                    waits=[tok, tb_m, tb_r], inc=sV)
                xs_rd[buf] = [t1]
                t2 = p.op("vector", lambda e, buf=buf: e.tensor_tensor(
                    tmpO[:, buf, :], tmp1[:], pb[7][:], ALU.mult),
                    waits=tO_rd[buf], inc=sV)
                ot = p.dma("sync", lambda e, c=c, g=g, buf=buf: e.dma_start(
                    yd.ap()[c, :, g * 512:(g + 1) * 512], tmpO[:, buf, :]),
                    waits=[t2], inc=sA)
                tsq = p.op("scalar", lambda e, buf=buf: e.activation(
                    sqb[:, buf, :], tmpO[:, buf, :], AF.Square),
                    waits=[t2] + ([sq_guard[buf]] if sq_guard[buf] else []),
                    inc=sS)
                mt = p.op("tensor", lambda e, c=c, g=g, buf=buf: e.matmul(
                    pb[2 + g][0:1, :], ones_cb[:], sqb[:, buf, :],
                    start=(c == 0), stop=(c == DC - 1)), waits=[tsq], inc=sP)
                sq_guard[buf] = mt
                rms_last[g] = mt
                tO_rd[buf] = [ot, tsq]
        y_written = (sA, sA.n)
        ap_guard = {0: [], 1: []}
        for g in range(NG):
            ts = p.op("scalar", lambda e, g=g: e.activation(
                r2row[:], pb[2 + g][0:1, :], AF.Sqrt, bias=eps6c[0:1, :],
                scale=1.0 / D), waits=[rms_last[g]] + row_guard, inc=sS)
            tr = p.op("vector", lambda e: e.reciprocal(r2row[:], r2row[:]),
                      waits=[ts], inc=sV)
            tb = bcast(p, pb[6], r2row[:], [tr] + row_guard)
            row_guard = [tb]
            for c in range(DC):
                buf = c % 2
                w = list(ap_guard[buf])
                if g == 0 and c < 2:
                    w.append(y_written)
                    w += xs_rd.get(buf, [])
                tok = p.dma("sync", lambda e, c=c, g=g, buf=buf: e.dma_start(
                    xs[:, buf, :], yd.ap()[c, :, g * 512:(g + 1) * 512]),
                    waits=w, inc=sA)
                at = p.op("vector", lambda e, c=c, g=g, buf=buf:
                          e.scalar_tensor_tensor(
                    act_bf[:, c, g * 512:(g + 1) * 512], xs[:, buf, :],
                    anwc[:, c:c + 1], pb[6][:], ALU.mult, ALU.mult),
                    waits=[tok, tb], inc=sV)
                ap_guard[buf] = [at]
        dump_map["t2"] = (act_bf, [128, DC, S], BF16)
        return p

    def ph_projB():
        p = mkprog()
        l1 = p.dma("gpsimd", lambda e: e.dma_start(
            wA[:], wqT.ap().rearrange("(c p) o -> p c o", p=128)), inc=sG)
        l2 = p.dma("gpsimd", lambda e: e.dma_start(
            wB[:], wkT.ap().rearrange("(c p) o -> p c o", p=128)), inc=sG)
        t_rdy = (sV, sV.n)
        ev_q = proj_fm(p, wA, act_bf,
                       lambda oc, g: qT[:, oc, g * 512:(g + 1) * 512],
                       l1, t_rdy, [])
        ev_k = proj_fm(p, wB, act_bf,
                       lambda oc, g: kT[:, oc, g * 512:(g + 1) * 512],
                       l2, t_rdy, ev_q[-4:])
        lv = p.dma("gpsimd", lambda e: e.dma_start(
            wA[:], wvT.ap().rearrange("(c p) o -> p c o", p=128)),
            waits=[ev_q[-1]], inc=sG)
        proj_tm_v(p, wA, act_bf, lv, t_rdy, ev_k[-4:])
        return p

    def ph_rope():
        p = mkprog()
        t0 = p.op("gpsimd", lambda e: e.iota(
            invf[:], [[0, 4], [1, 32]], channel_multiplier=0, base=0,
            allow_small_or_imprecise_dtypes=True), inc=sGc)
        t2 = p.op("vector", lambda e: e.tensor_scalar(
            invf[:], invf[:], -math.log(10000.0) / 32.0, None,
            ALU.mult), waits=[t0], inc=sV)
        t3 = p.op("scalar", lambda e: e.activation(invf[:], invf[:], AF.Exp),
                  waits=[t2], inc=sS)
        def range_reduce(first_waits, src_psum):
            c8 = 2 * PI * 256.0
            t_m = p.op("vector", lambda e: e.tensor_scalar(
                tmp2[:], src_psum, c8, None, ALU.is_ge),
                waits=first_waits, inc=sV)
            p.op("vector", lambda e: e.tensor_scalar(
                tmp2[:], tmp2[:], c8, None, ALU.mult), inc=sV)
            p.op("vector", lambda e: e.tensor_tensor(
                tmp1[:], src_psum, tmp2[:], ALU.subtract), inc=sV)
            for j in range(7, -1, -1):
                cj = 2 * PI * (2.0 ** j)
                p.op("vector", lambda e, cj=cj: e.tensor_scalar(
                    tmp2[:], tmp1[:], cj, None, ALU.is_ge), inc=sV)
                p.op("vector", lambda e, cj=cj: e.tensor_scalar(
                    tmp2[:], tmp2[:], cj, None, ALU.mult), inc=sV)
                tl = p.op("vector", lambda e: e.tensor_tensor(
                    tmp1[:], tmp1[:], tmp2[:], ALU.subtract), inc=sV)
            return tl

        prev_te = None
        prev_cos = None
        for g in range(NG):
            ti = p.op("gpsimd", lambda e, g=g: e.iota(
                vrow[:], [[1, 512]], base=g * 512, channel_multiplier=0,
                allow_small_or_imprecise_dtypes=True),
                waits=([prev_cos] if prev_cos else []), inc=sGc)
            tf = p.op("tensor", lambda e: e.matmul(
                pb[0][:], invf[:], vrow[:], start=True, stop=True),
                waits=[ti, t3] + ([prev_cos] if prev_cos else []), inc=sP)
            ta = range_reduce([tf] + ([prev_te] if prev_te else []), pb[0][:])
            tb = p.op("scalar", lambda e: e.activation(
                tmp2[:], tmp1[:], AF.Sin, bias=npic[:]), waits=[ta], inc=sS)
            tc = p.op("vector", lambda e, g=g: e.scalar_tensor_tensor(
                sinT[:, g * 512:(g + 1) * 512], tmp2[:], nsgn[:], tmp2[:],
                ALU.mult, ALU.bypass), waits=[tb], inc=sV)
            td0 = p.op("vector", lambda e: e.tensor_scalar(
                tmp1[:], tmp1[:], PI / 2, None, ALU.add),
                waits=[tb], inc=sV)
            p.op("vector", lambda e: e.tensor_scalar(
                tmp2[:], tmp1[:], 2 * PI, None, ALU.is_ge), inc=sV)
            p.op("vector", lambda e: e.tensor_scalar(
                tmp2[:], tmp2[:], 2 * PI, None, ALU.mult), inc=sV)
            td = p.op("vector", lambda e: e.tensor_tensor(
                tmp1[:], tmp1[:], tmp2[:], ALU.subtract), inc=sV)
            te = p.op("scalar", lambda e: e.activation(
                tmp2[:], tmp1[:], AF.Sin, bias=npic[:]), waits=[td, tc],
                inc=sS)
            prev_te = te
            prev_cos = p.op("vector", lambda e, g=g: e.tensor_scalar(
                cosT[:, g * 512:(g + 1) * 512], tmp2[:], -1.0, None,
                ALU.mult), waits=[te], inc=sV)
        tab_rdy = prev_cos
        last_ta = None
        for src in (qT, kT):
            for c in range(DC):
                rd = None
                for (d0, s0) in ((0, 32), (32, 0), (64, 96), (96, 64)):
                    rd = p.dma("sync", lambda e, c=c, d0=d0, s0=s0, src=src:
                               e.dma_start(rot[d0:d0 + 32, :],
                                           src[s0:s0 + 32, c, :]),
                               waits=([last_ta] if (last_ta and d0 == 0)
                                      else []), inc=sA)
                for g in range(NG):
                    ta = p.op("vector", lambda e, g=g: e.tensor_tensor(
                        rpA[:], rot[:, g * 512:(g + 1) * 512],
                        sinT[:, g * 512:(g + 1) * 512], ALU.mult),
                        waits=[rd, tab_rdy], inc=sV)
                    last_ta = ta
                    p.op("vector", lambda e, c=c, g=g, src=src:
                         e.tensor_tensor(
                        rpB[:], src[:, c, g * 512:(g + 1) * 512],
                        cosT[:, g * 512:(g + 1) * 512], ALU.mult), inc=sV)
                    p.op("vector", lambda e, c=c, g=g, src=src:
                         e.tensor_tensor(
                        src[:, c, g * 512:(g + 1) * 512], rpA[:], rpB[:],
                        ALU.add), inc=sV)
        dump_map["ropeq"] = (qT, [128, DC, S], BF16)
        dump_map["cosT"] = (cosT, [128, S], F32)
        dump_map["sinT"] = (sinT, [128, S], F32)
        return p

    def ph_woB():
        p = mkprog()
        y_rdy = (sA, sA.n)
        lo = p.dma("gpsimd", lambda e: e.dma_start(
            wA[:], woT.ap().rearrange("(c p) o -> p c o", p=128)), inc=sG)
        ctx_rdy = (sV, sV.n)
        evac = []
        xs_rd = {}
        gi = 0
        banks = [pb[0], pb[1], pb[2], pb[3]]
        for oc in range(DC):
            for g in range(NG):
                bank = banks[gi % 4]
                waits = [lo, ctx_rdy]
                if gi >= 4:
                    waits.append(evac[gi - 4])
                for ic in range(DC):
                    tok = p.op("tensor", lambda e, ic=ic, oc=oc, g=g,
                               bank=bank: e.matmul(
                        bank[:], wA[:, ic, oc * 128:(oc + 1) * 128],
                        ctxT[:, ic, g * 512:(g + 1) * 512],
                        start=(ic == 0), stop=(ic == DC - 1)),
                        waits=(waits if ic == 0 else []), inc=sP)
                buf = gi % 2
                w2_ = [xs_rd.get(buf)] if xs_rd.get(buf) else []
                if gi < 2:
                    w2_.append(y_rdy)
                yt = p.dma("sync", lambda e, oc=oc, g=g, buf=buf: e.dma_start(
                    xs[:, buf, :], yd.ap()[oc, :, g * 512:(g + 1) * 512]),
                    waits=w2_, inc=sA)
                et = p.op("vector", lambda e, oc=oc, g=g, bank=bank, buf=buf:
                          e.tensor_tensor(
                    h_sb[:, oc, g * 512:(g + 1) * 512], bank[:],
                    xs[:, buf, :], ALU.add), waits=[tok, yt], inc=sV)
                xs_rd[buf] = et
                evac.append(et)
                gi += 1
        dump_map["h_sb"] = (h_sb, [128, DC, S], F32)
        return p

    def ph_gate():
        p = mkprog()
        h_rdy = (sV, sV.n)
        sq_rd = {0: None, 1: None}
        rms_tok = {}
        for g in range(NG):
            last = None
            for c in range(DC):
                w = [h_rdy] if (g == 0 and c == 0) else []
                if sq_rd[c % 2]:
                    w.append(sq_rd[c % 2])
                tsq = p.op("scalar", lambda e, c=c, g=g: e.activation(
                    sqb[:, c % 2, :], h_sb[:, c, g * 512:(g + 1) * 512],
                    AF.Square), waits=w, inc=sS)
                last = p.op("tensor", lambda e, c=c, g=g: e.matmul(
                    pb[2 + g][0:1, :], ones_cb[:], sqb[:, c % 2, :],
                    start=(c == 0), stop=(c == DC - 1)), waits=[tsq], inc=sP)
                sq_rd[c % 2] = last
            rms_tok[g] = last
        row_guard = None
        ap_last = None
        for g in range(NG):
            ts = p.op("scalar", lambda e, g=g: e.activation(
                r2row[:], pb[2 + g][0:1, :], AF.Sqrt, bias=eps6c[0:1, :],
                scale=1.0 / D),
                waits=[rms_tok[g]] + ([row_guard] if row_guard else []),
                inc=sS)
            tr = p.op("vector", lambda e: e.reciprocal(r2row[:], r2row[:]),
                      waits=[ts], inc=sV)
            tb = bcast(p, pb[6], r2row[:],
                       [tr] + ([ap_last] if ap_last else []))
            row_guard = tb
            for c in range(DC):
                ap_last = p.op("vector", lambda e, c=c, g=g:
                               e.scalar_tensor_tensor(
                    act_bf[:, c, g * 512:(g + 1) * 512],
                    h_sb[:, c, g * 512:(g + 1) * 512],
                    mnwc[:, c:c + 1], pb[6][:], ALU.mult, ALU.mult),
                    waits=[tb], inc=sV)
        tn_rdy = ap_last
        dump_map["tn"] = (act_bf, [128, DC, S], BF16)
        ev = []
        for ti in range(NT):
            bank = pb[ti % 2]
            w = [tn_rdy]
            if ti >= 2:
                w.append(ev[ti - 2])
            for ic in range(DC):
                tok = p.op("tensor", lambda e, ti=ti, ic=ic, bank=bank:
                           e.matmul(
                    bank[:, 0:8], act_bf[:, ic, ti * 128:(ti + 1) * 128],
                    gw_b[:, ic, :], start=(ic == 0), stop=(ic == DC - 1)),
                    waits=(w if ic == 0 else []), inc=sP)
            et = p.op("vector", lambda e, ti=ti, bank=bank: e.tensor_copy(
                logits[:, ti, :], bank[:, 0:8]), waits=[tok], inc=sV)
            ev.append(et)
        t2_prev = None
        for ti in range(NT):
            tmx = p.op("vector", lambda e, ti=ti: e.reduce_max(
                rw1[:], logits[:, ti, :], axis=AXX),
                waits=[ev[ti]] + ([t2_prev] if t2_prev else []), inc=sV)
            p.op("vector", lambda e, ti=ti: e.scalar_tensor_tensor(
                n1s[:], logits[:, ti, :], rw1[:], logits[:, ti, :],
                ALU.subtract, ALU.bypass), waits=[tmx], inc=sV)
            p.op("vector", lambda e: e.tensor_scalar(
                n1s[:], n1s[:], 0.0, None, ALU.is_ge), inc=sV)
            p.op("vector", lambda e: e.tensor_scalar(
                scr8[:], n1s[:], 1e30, None, ALU.mult), inc=sV)
            p.op("vector", lambda e, ti=ti: e.tensor_tensor(
                L2s[:], logits[:, ti, :], scr8[:], ALU.subtract), inc=sV)
            tm2 = p.op("vector", lambda e: e.reduce_max(
                rw2[:], L2s[:], axis=AXX), inc=sV)
            p.op("vector", lambda e: e.scalar_tensor_tensor(
                n2s[:], L2s[:], rw2[:], L2s[:], ALU.subtract, ALU.bypass),
                waits=[tm2], inc=sV)
            p.op("vector", lambda e: e.tensor_scalar(
                n2s[:], n2s[:], 0.0, None, ALU.is_ge), inc=sV)
            td = p.op("vector", lambda e: e.tensor_tensor(
                d12[:], rw1[:], rw2[:], ALU.subtract), inc=sV)
            p.op("scalar", lambda e: e.activation(
                rw1[:], d12[:], AF.Sigmoid), waits=[td], inc=sS)
            t2 = p.op("scalar", lambda e: e.activation(
                rw2[:], d12[:], AF.Sigmoid, scale=-1.0), inc=sS)
            t2_prev = t2
            t3 = p.op("vector", lambda e: e.scalar_tensor_tensor(
                w2s[:], n1s[:], rw1[:], n1s[:], ALU.mult, ALU.bypass),
                waits=[t2], inc=sV)
            p.op("vector", lambda e, ti=ti: e.scalar_tensor_tensor(
                wtm[:, ti, :], n2s[:], rw2[:], w2s[:], ALU.mult, ALU.add),
                waits=[t3], inc=sV)
        wt = (sV, sV.n)
        tz = p.op("vector", lambda e: e.memset(w32a[:], 0.0), waits=[wt],
                  inc=sV)
        tc_ = p.op("vector", lambda e: e.tensor_copy(
            w32a3[:, :, 0:8], wtm[:]), waits=[tz], inc=sV)
        tt_ = p.op("vector", lambda e: e.transpose(w32b[:], w32a[:]),
                   waits=[tc_], inc=sV)
        wem3 = wem[:].rearrange("e (tt a i) -> e tt a i", a=4, i=32)
        for a in range(4):
            p.dma("gpsimd", lambda e, a=a: e.dma_start(
                wem3[:, :, a, :], w32b3[32 * a:32 * a + 8, :, :]),
                waits=[tt_], inc=sG)
        p.dma("gpsimd", lambda e: e.dma_start(b2b[:], b2_d.ap()), inc=sG)
        dump_map["wtm"] = (wtm, [128, NT, 8], F32)
        return p

    def ph_moe():
        p = mkprog()
        wem_rdy = (sG, sG.n)
        tn_rdy = (sV, sV.n)
        pre = []
        gi = 0
        for dc in range(DC):
            for g in range(NG):
                bank = pb[gi % 2]
                w = [wem_rdy]
                if gi >= 2:
                    w.append(pre[gi - 2])
                tok = p.op("tensor", lambda e, dc=dc, g=g, bank=bank:
                           e.matmul(
                    bank[:], b2b[:, dc * 128:(dc + 1) * 128],
                    wem[:, g * 512:(g + 1) * 512], start=True, stop=True),
                    waits=w, inc=sP)
                et = p.op("vector", lambda e, dc=dc, g=g, bank=bank:
                          e.tensor_tensor(
                    h_sb[:, dc, g * 512:(g + 1) * 512], bank[:],
                    h_sb[:, dc, g * 512:(g + 1) * 512], ALU.add),
                    waits=[tok], inc=sV)
                pre.append(et)
                gi += 1
        prev_acc = pre[-1]
        blk_end = []
        gelu_bank_guard = {0: None, 1: None, 2: None, 3: None}
        tmpg_guard = {0: None, 1: None, 2: None, 3: None}
        hid_guard = {}
        mm2_evac = []
        acc_toks = []
        acc_toks_by = {"pool": [], "dve": []}
        prev_wb = []
        for e_i in range(E):
            wrow_tok = p.dma("sync", lambda e, e_i=e_i: e.dma_start(
                wrow[:], wem[e_i:e_i + 1, :]),
                waits=[wem_rdy] + prev_wb, inc=sA)
            wb_toks = {}
            for g in range(NG):
                mmt = p.op("tensor", lambda e, g=g: e.matmul(
                    pb[4 + g][:], ones_b[:],
                    wrow[:, g * 512:(g + 1) * 512],
                    start=True, stop=True),
                    waits=[wrow_tok, prev_acc], inc=sP)
                wb_toks[g] = p.op("vector", lambda e, g=g: e.tensor_copy(
                    wb_sb[:, g, :], pb[4 + g][:]), waits=[mmt], inc=sV)
            prev_wb = list(wb_toks.values())
            for q8 in range(8):
                j = e_i * 8 + q8
                buf = q8 % 2
                wl = [blk_end[j - 2]] if j >= 2 else []
                lt1 = p.dma("gpsimd", lambda e, e_i=e_i, q8=q8, buf=buf:
                            e.dma_start(
                    w1c[:, buf, :, :],
                    w1T.ap()[e_i, :, q8 * 512:(q8 + 1) * 512].rearrange(
                        "(c p) h -> p c h", p=128)), waits=wl, inc=sG)
                lt2 = p.dma("gpsimd", lambda e, e_i=e_i, q8=q8, buf=buf:
                            e.dma_start(
                    w2c[:, buf, :, :],
                    w2T.ap()[e_i, q8 * 512:(q8 + 1) * 512, :].rearrange(
                        "(c p) d -> p c d", p=128)), inc=sG)
                last_mm2 = None
                gelu_toks = {}

                def emit_mm1(g, j=j, buf=buf, lt1=lt1, e_i=e_i, q8=q8):
                    gb = g % 2
                    toks = []
                    for hc in range(4):
                        bank = pb[[0, 1, 4, 5][hc]]
                        w = [lt1]
                        if j == 0 and g == 0 and hc == 0:
                            w += [tn_rdy, prev_acc]
                        if q8 == 0 and g == 0 and hc >= 2:
                            w.append(wb_toks[hc - 2])
                        if gelu_bank_guard[hc]:
                            w.append(gelu_bank_guard[hc])
                        for ic in range(DC):
                            tok = p.op("tensor", lambda e, hc=hc, ic=ic, g=g,
                                       buf=buf, bank=bank: e.matmul(
                                bank[:],
                                w1c[:, buf, ic, hc * 128:(hc + 1) * 128],
                                act_bf[:, ic, g * 512:(g + 1) * 512],
                                start=(ic == 0), stop=(ic == DC - 1)),
                                waits=(w if ic == 0 else []), inc=sP)
                        hw_ = [tok]
                        if tmpg_guard[hc]:
                            hw_.append(tmpg_guard[hc])
                        ht = p.op("scalar", lambda e, hc=hc,
                                  e_i=e_i, q8=q8, bank=bank: e.activation(
                            tmpG[:, hc, :], bank[:], AF.Gelu,
                            bias=b1c[:, e_i, q8 * 4 + hc:q8 * 4 + hc + 1]),
                            waits=hw_, inc=sS)
                        gelu_bank_guard[hc] = ht
                        sw_ = [ht, wb_toks[g]]
                        if hid_guard.get((gb, hc)):
                            sw_.append(hid_guard[(gb, hc)])
                        st_ = p.op("vector", lambda e, hc=hc, gb=gb, g=g:
                                   e.tensor_tensor(
                            hid[:, gb, hc, :], tmpG[:, hc, :],
                            wb_sb[:, g, :], ALU.mult), waits=sw_, inc=sV)
                        tmpg_guard[hc] = st_
                        toks.append(st_)
                    gelu_toks[g] = toks

                def emit_mm2(g, buf=buf, lt2=lt2, q8=q8):
                    nonlocal last_mm2
                    gb = g % 2
                    for dc in range(DC):
                        bank = pb[[2, 3, 6, 7][dc % 4]]
                        w = [lt2] + gelu_toks[g]
                        if q8 == 0 and g == 0 and dc % 4 >= 2:
                            w.append(wb_toks[dc % 4])
                        if len(mm2_evac) >= 4:
                            w.append(mm2_evac[-4])
                        for hc in range(4):
                            tok = p.op("tensor", lambda e, dc=dc, hc=hc,
                                       gb=gb, buf=buf, bank=bank: e.matmul(
                                bank[:],
                                w2c[:, buf, hc, dc * 128:(dc + 1) * 128],
                                hid[:, gb, hc, :],
                                start=(hc == 0), stop=(hc == 3)),
                                waits=(w if hc == 0 else []), inc=sP)
                        last_mm2 = tok
                        for hc in range(4):
                            hid_guard[(gb, hc)] = tok
                        t2 = p.op("vector", lambda e, dc=dc, g=g, bank=bank:
                                  e.tensor_tensor(
                            h_sb[:, dc, g * 512:(g + 1) * 512], bank[:],
                            h_sb[:, dc, g * 512:(g + 1) * 512], ALU.add),
                            waits=[tok], inc=sV)
                        mm2_evac.append(t2)
                        acc_toks.append(t2)

                emit_mm1(0)
                for g in range(NG):
                    if g + 1 < NG:
                        emit_mm1(g + 1)
                    emit_mm2(g)
                blk_end.append(last_mm2)
            prev_acc = acc_toks[-1]
        dump_map["moe_h"] = (h_sb, [128, DC, S], F32)
        return p

    def ph_final():
        p = mkprog()
        h_rdy = (sGc, sGc.n)
        h_rdy2 = (sV, sV.n)
        x1_rdy = (sA, sA.n)
        sq_guard = [None, None]
        row_guard = [None, None]
        xs_rd = {}
        tO_rd = {}
        last_out = None
        for g in range(NG):
            def src(c, g=g):
                return h_sb[:, c, g * 512:(g + 1) * 512], None
            st, _ = ln_stats(p, src, False, pb[0], pb[1],
                             [h_rdy, h_rdy2] if g == 0 else [], sq_guard)
            rt = ln_rows(p, pb[0], pb[1], st, eps5c, row_guard)
            tb_m = bcast(p, pb[6], mrow[:], [rt])
            tb_r = bcast(p, pb[7], r2row[:], [rt])
            row_guard = [tb_m, tb_r]
            for c in range(DC):
                buf = c % 2
                w = [xs_rd.get(buf)] if xs_rd.get(buf) else []
                if g == 0 and c < 2:
                    w.append(x1_rdy)
                tok = p.dma("sync", lambda e, c=c, g=g, buf=buf: e.dma_start(
                    xs[:, buf, :], x1d.ap()[c, :, g * 512:(g + 1) * 512]),
                    waits=w, inc=sA)
                t1 = p.op("vector", lambda e, c=c, g=g: e.tensor_tensor(
                    tmp1[:], h_sb[:, c, g * 512:(g + 1) * 512], pb[6][:],
                    ALU.subtract), waits=[tb_m, tb_r], inc=sV)
                t2 = p.op("vector", lambda e: e.tensor_tensor(
                    tmp2[:], tmp1[:], pb[7][:], ALU.mult), inc=sV)
                wO = [tO_rd.get(buf)] if tO_rd.get(buf) else []
                t3 = p.op("vector", lambda e, buf=buf: e.tensor_tensor(
                    tmpO[:, buf, :], tmp2[:], xs[:, buf, :], ALU.add),
                    waits=[tok] + wO, inc=sV)
                xs_rd[buf] = t3
                ot = p.dma("sync", lambda e, c=c, g=g, buf=buf: e.dma_start(
                    outT.ap()[c * 128:(c + 1) * 128,
                              g * 512:(g + 1) * 512], tmpO[:, buf, :]),
                    waits=[t3], inc=sO)
                tO_rd[buf] = ot
                last_out = ot
        p.op("sync", lambda e: e.nop(), waits=[last_out])
        return p

    phases = [
        ("init", ph_init),
        ("ln1", ph_ln1),
        ("projA", ph_projA),
        ("cboundA", lambda: ph_cbound(0)),
        ("attnA", lambda: ph_attn(False)),
        ("woA", ph_woA),
        ("ln3", ph_ln3),
        ("projB", ph_projB),
        ("rope", ph_rope),
        ("cboundB", lambda: ph_cbound(1)),
        ("attnB", lambda: ph_attn(True)),
        ("woB", ph_woB),
        ("gate", ph_gate),
        ("moe", ph_moe),
        ("final", ph_final),
    ]
    for name, fn in phases:
        prog = fn()
        prog.emit(nc)
        if stop_after is not None and name == stop_after:
            break

    if dump is not None:
        ap_t, shp, dt = dump_map[dump]
        flat = int(np.prod(shp[1:]))
        dbg = nc.dram_tensor("dbg", [shp[0], flat], dt, kind="ExternalOutput")
        pd = mkprog()
        src_ap = ap_t[:] if len(shp) == 2 else ap_t[:].rearrange(
            "p ... -> p (...)")
        pd.dma("sync", lambda e: e.dma_start(dbg.ap(), src_ap), inc=sO)
        pd.op("sync", lambda e: e.nop(), waits=[(sO, sO.n)])
        pd.emit(nc)

    nc.finalize()
    return nc


def _host_pack(inputs):
    g = {k: np.asarray(v) for k, v in inputs.items()}
    asc = np.ascontiguousarray

    def colpack(v):
        return asc(v.reshape(8, 128).T)
    common = {
        "aqT": asc(g["L0_attn_Wq"].T), "akT": asc(g["L0_attn_Wk"].T),
        "avT": asc(g["L0_attn_Wv"].T), "aoT": asc(g["L0_attn_Wo"].T),
        "bqc": colpack(g["L0_attn_bq"]), "bkc": colpack(g["L0_attn_bk"]),
        "bvc": colpack(g["L0_attn_bv"]), "boc": colpack(g["L0_attn_bo"]),
        "anwc": colpack(g["L0_ffn_attn_norm_w"]),
        "mnwc": colpack(g["L0_ffn_moe_norm_w"]),
        "wqT": asc(g["L0_ffn_wq"].T), "wkT": asc(g["L0_ffn_wk"].T),
        "wvT": asc(g["L0_ffn_wv"].T), "woT": asc(g["L0_ffn_wo"].T),
        "gwT": asc(g["L0_ffn_gate_w"].T),
        "w1T": asc(g["L0_ffn_w1"].transpose(0, 2, 1)),
        "w2T": asc(g["L0_ffn_w2"].transpose(0, 2, 1)),
        "b1c": asc(g["L0_ffn_b1"].reshape(E, 32, 128).transpose(2, 0, 1)),
        "b2": asc(g["L0_ffn_b2"]),
    }
    x = g["L0_attn_inputs"]
    in_maps = []
    for b in range(8):
        m = dict(common)
        m["xT"] = asc(x[b].T)
        in_maps.append(m)
    return in_maps


def kernel(**inputs):
    in_maps = _host_pack(inputs)
    nc = build_program()
    res = run_bass_kernel_spmd(nc, in_maps, core_ids=list(range(8)))
    out = np.stack([np.ascontiguousarray(np.asarray(r["outT"]).T)
                    for r in res.results], axis=0)
    return out.astype(np.float32)
